# version 28
# baseline (speedup 1.0000x reference)
import contextlib
import numpy as np
import concourse.bass as bass
import concourse.mybir as mybir
from concourse.bass_utils import run_bass_kernel_spmd

F32 = mybir.dt.float32
BF16 = mybir.dt.bfloat16
I32 = mybir.dt.int32
AF = mybir.ActivationFunctionType
ALU = mybir.AluOpType
AX = mybir.AxisListType

LOGIT_LO = float(np.log(np.float64(0.8) / 0.2))
LOGIT_HI = float(np.log(np.float64(0.9995) / 0.0005))
ATT_EPS = 1e-6
LN_EPS = 1e-5


class Cfg:
    def __init__(self, L=2048, DM=2048, FH=5632, flat=False):
        self.L, self.DM, self.FH = L, DM, FH
        self.flat = flat
        self.dbg = False
        self.H = DM // 128
        self.C = 128
        self.NC = L // 128
        self.MT = L // 128
        self.KT = DM // 128
        self.NB = DM // 512
        self.TB = L // 512
        self.HG = min(4, self.H)
        self.NG = self.H // self.HG
        self.FS = 256
        self.NS = FH // self.FS
        self.FKT = FH // 128
        self.DDS = 512
        self.NDS = DM // self.DDS


class Buf:
    __slots__ = ("name", "writer", "readers")

    def __init__(self, name):
        self.name = name
        self.writer = None
        self.readers = []


ENGINES = ("pe", "ve", "se", "ge", "sp")


class Prog:
    def __init__(self, nc, stack):
        self.nc = nc
        self.stack = stack
        self.streams = {e: [] for e in ENGINES}
        self.sems = {}
        self.counts = {}
        self.last_wait = {e: {} for e in ENGINES}
        for e in ENGINES:
            self._sem(e)

    def _sem(self, name):
        if name not in self.sems:
            self.sems[name] = self.stack.enter_context(self.nc.semaphore(name))
            self.counts[name] = 0
        return self.sems[name]

    def buf(self, name):
        return Buf(name)

    def bufs(self, name, n):
        return [Buf(f"{name}{i}") for i in range(n)]

    def _edges(self, eng, reads, writes):
        need = {}
        for b in reads:
            if b.writer:
                s, v = b.writer
                if v > need.get(s, 0):
                    need[s] = v
        for b in writes:
            if b.writer:
                s, v = b.writer
                if v > need.get(s, 0):
                    need[s] = v
            for s, v in b.readers:
                if v > need.get(s, 0):
                    need[s] = v
        lw = self.last_wait[eng]
        waits = []
        for s, v in need.items():
            if lw.get(s, 0) < v:
                lw[s] = v
                waits.append((self.sems[s], v))
        return waits

    def _commit(self, tag, reads, writes):
        for b in writes:
            b.writer = tag
            b.readers = []
        for b in reads:
            b.readers.append(tag)

    def op(self, eng, fn, reads=(), writes=(), sem=None, n_inc=1):
        sem_name = sem or eng
        self._sem(sem_name)
        waits = self._edges(eng, reads, writes)
        self.counts[sem_name] += n_inc
        val = self.counts[sem_name]
        semh = self.sems[sem_name]

        def thunk(engine, waits=waits, fn=fn, semh=semh, n_inc=n_inc):
            for sh, v in waits:
                engine.wait_ge(sh, v)
            fn(engine).then_inc(semh, n_inc)

        self.streams[eng].append(thunk)
        self._commit((sem_name, val), reads, writes)

    def mm_group(self, fns, reads=(), writes=()):
        waits = self._edges("pe", reads, writes)
        self.counts["pe"] += 1
        val = self.counts["pe"]
        semh = self.sems["pe"]

        def thunk(engine, waits=waits, fns=tuple(fns), semh=semh):
            for sh, v in waits:
                engine.wait_ge(sh, v)
            last = None
            for f in fns:
                last = f(engine)
            last.then_inc(semh, 1)

        self.streams["pe"].append(thunk)
        self._commit(("pe", val), reads, writes)

    def dma(self, queue, fn, reads=(), writes=(), sem="d"):
        self.op(queue, fn, reads=reads, writes=writes, sem=sem, n_inc=16)

    def dma_group(self, queue, fns, reads=(), writes=(), sem="d"):
        self._sem(sem)
        waits = self._edges(queue, reads, writes)
        semh = self.sems[sem]
        self.counts[sem] += 16 * len(fns)
        val = self.counts[sem]

        def thunk(engine, waits=waits, fns=tuple(fns), semh=semh):
            for sh, v in waits:
                engine.wait_ge(sh, v)
            for f in fns:
                f(engine).then_inc(semh, 16)

        self.streams[queue].append(thunk)
        self._commit((sem, val), reads, writes)

    def barrier(self):
        snap = [(s, v) for s, v in self.counts.items() if v > 0]
        for eng in ENGINES:
            lw = self.last_wait[eng]
            pend = [(self.sems[s], v) for s, v in snap if lw.get(s, 0) < v]
            for s, v in snap:
                if lw.get(s, 0) < v:
                    lw[s] = v
            if pend:
                def thunk(engine, pend=tuple(pend)):
                    for sh, v in pend:
                        engine.wait_ge(sh, v)
                self.streams[eng].append(thunk)

    def final_wait(self, eng, sem_name):
        v = self.counts[sem_name]
        semh = self.sems[sem_name]

        def thunk(engine, semh=semh, v=v):
            engine.wait_ge(semh, v)

        self.streams[eng].append(thunk)

    def replay(self, blk):
        for eng, meth in (("sp", blk.sync), ("pe", blk.tensor), ("ve", blk.vector),
                          ("se", blk.scalar), ("ge", blk.gpsimd)):
            stream = self.streams[eng]

            def body(engine, stream=stream):
                for t in stream:
                    t(engine)

            meth(body)


def build_program(cfg: Cfg):
    nc = bass.Bass("TRN2", target_bir_lowering=False, debug=False)
    L, DM, FH, H = cfg.L, cfg.DM, cfg.FH, cfg.H
    C, NC, MT, KT, NB, TB = cfg.C, cfg.NC, cfg.MT, cfg.KT, cfg.NB, cfg.TB
    HG, NG, FS, NS, FKT, DDS, NDS = cfg.HG, cfg.NG, cfg.FS, cfg.NS, cfg.FKT, cfg.DDS, cfg.NDS

    x_h = nc.dram_tensor("x", [L, DM], F32, kind="ExternalInput").ap()
    wq_h = nc.dram_tensor("wq", [DM, DM], F32, kind="ExternalInput").ap()
    wk_h = nc.dram_tensor("wk", [DM, DM], F32, kind="ExternalInput").ap()
    wv_h = nc.dram_tensor("wv", [DM, DM], F32, kind="ExternalInput").ap()
    wg_h = nc.dram_tensor("wg", [DM, H], F32, kind="ExternalInput").ap()
    bg_h = nc.dram_tensor("bg", [H, 1], F32, kind="ExternalInput").ap()
    wo_h = nc.dram_tensor("wo", [DM, DM], F32, kind="ExternalInput").ap()
    bo_h = nc.dram_tensor("bo", [1, DM], F32, kind="ExternalInput").ap()
    wqg_h = nc.dram_tensor("wqg", [FH, DM], I32, kind="ExternalInput").ap()
    sg_h = nc.dram_tensor("sg", [128, FH // 128], F32, kind="ExternalInput").ap()
    wqu_h = nc.dram_tensor("wqu", [FH, DM], I32, kind="ExternalInput").ap()
    su_h = nc.dram_tensor("su", [128, FH // 128], F32, kind="ExternalInput").ap()
    wqd_h = nc.dram_tensor("wqd", [DM, FH], I32, kind="ExternalInput").ap()
    sd_h = nc.dram_tensor("sd", [128, DM // 128], F32, kind="ExternalInput").ap()
    rc_h = nc.dram_tensor("rc", [L, 64], F32, kind="ExternalInput").ap()
    rs_h = nc.dram_tensor("rs", [L, 64], F32, kind="ExternalInput").ap()
    y_h = nc.dram_tensor("y", [L, DM], F32, kind="ExternalOutput").ap()

    dbgk = "ExternalOutput" if cfg.dbg else "Internal"
    xn_sp = nc.dram_tensor("xn_sp", [L, DM], BF16, kind=dbgk).ap()
    qt_sp = nc.dram_tensor("qt_sp", [L, DM], BF16, kind=dbgk).ap()
    kt_sp = nc.dram_tensor("kt_sp", [L, DM], BF16, kind=dbgk).ap()
    kh_sp = nc.dram_tensor("kh_sp", [L, DM], BF16, kind=dbgk).ap()
    v_sp = nc.dram_tensor("v_sp", [L, DM], BF16, kind=dbgk).ap()
    yt_sp = nc.dram_tensor("yt_sp", [NC, 128, DM], BF16, kind=dbgk).ap()
    s_sp = nc.dram_tensor("s_sp", [L, DM], F32, kind=dbgk).ap()
    wg_sp = nc.dram_tensor("wg_sp", [FH, DM], BF16, kind=dbgk).ap()
    wu_sp = nc.dram_tensor("wu_sp", [FH, DM], BF16, kind=dbgk).ap()
    wd_sp = nc.dram_tensor("wd_sp", [DM, FH], BF16, kind=dbgk).ap()
    gu_sp = nc.dram_tensor("gu_sp", [FKT, 128, L], BF16, kind=dbgk).ap()

    with contextlib.ExitStack() as st:
        P = Prog(nc, st)

        def sb(name, shape, dt, stack=st, **kw):
            if cfg.flat:
                stack = st
            return stack.enter_context(nc.sbuf_tensor(name, shape, dt, **kw))

        onesf = sb("onesf", [128, 128], F32)
        onesb = sb("onesb", [128, 8], BF16)
        sg_sb = sb("sg_sb", [128, FH // 128], F32)
        su_sb = sb("su_sb", [128, FH // 128], F32)
        sd_sb = sb("sd_sb", [128, DM // 128], F32)
        bg_sb = sb("bg_sb", [H, 1], F32)
        stA = st.enter_context(contextlib.ExitStack())
        identf = sb("identf", [128, 128], F32, stack=stA)
        maskf = sb("maskf", [128, 128], F32, stack=stA)

        aC_all = sb("aC_all", [128, NC * H], F32, stack=stA)
        pb = [st.enter_context(nc.psum_tensor(f"pb{i}", [128, 512], F32)) for i in range(8)]
        pbB = P.bufs("pb", 8)

        constsB = P.buf("consts")
        beta_cB = [Buf(f"beta_c{c}") for c in range(L // 128)]
        identB = P.buf("ident")
        maskB = P.buf("mask")
        selB = P.buf("sel")
        ssgnB = P.buf("ssgn")
        betaB = P.buf("beta")
        xn_spB = P.bufs("xn_sp", MT)
        qt_spB = [[P.buf(f"qtsp{m}_{n}") for n in range(NB)] for m in range(MT)]
        kt_spB = [[P.buf(f"ktsp{m}_{n}") for n in range(NB)] for m in range(MT)]
        kh_spB = [[P.buf(f"khsp{m}_{n}") for n in range(NB)] for m in range(MT)]
        v_spB = [[P.buf(f"vsp{m}_{n}") for n in range(NB)] for m in range(MT)]
        yt_spB = [[P.buf(f"ytsp{c}_{g}") for g in range(NG)] for c in range(NC)]
        s_spB = [[P.buf(f"ssp{m}_{hf}") for hf in range(2)] for m in range(MT)]
        RPC_GU = 128 * ((4 * DM) // DM)
        RPC_D = 128 * ((4 * DM) // FH)
        wgsp_B = P.bufs("wgsp", FH // RPC_GU)
        wusp_B = P.bufs("wusp", FH // RPC_GU)
        wdsp_B = P.bufs("wdsp", DM // RPC_D)
        gusp_B = [[P.buf(f"gusp{k}_{t}") for t in range(TB)] for k in range(FKT)]

        P.op("ge", lambda e: e.iota(iraw[:], pattern=[[1, 128]], base=0, channel_multiplier=-1,
                                    allow_small_or_imprecise_dtypes=True), writes=[identB])
        P.op("ve", lambda e: e.tensor_scalar(out=identf[:], in0=iraw[:], scalar1=0.0, scalar2=None,
                                             op0=ALU.is_equal), reads=[identB], writes=[identB])
        P.op("ge", lambda e: e.iota(maskf[:], pattern=[[1, 128]], base=0, channel_multiplier=-1,
                                    allow_small_or_imprecise_dtypes=True), writes=[maskB])
        P.op("ve", lambda e: e.tensor_scalar(out=maskf[:], in0=maskf[:], scalar1=-0.5, scalar2=None,
                                             op0=ALU.is_gt), reads=[maskB], writes=[maskB])
        P.op("ve", lambda e: e.memset(onesf[:], 1.0), writes=[constsB])
        P.op("ve", lambda e: e.memset(onesb[:], 1.0), writes=[constsB])
        P.op("ve", lambda e: e.memset(zrow[:], 0.0), writes=[constsB])
        for h in range(16):
            P.op("ge", lambda e, h=h: e.iota(selt[:, h * 128:(h + 1) * 128], pattern=[[0, 128]],
                                             base=-h, channel_multiplier=1,
                                             allow_small_or_imprecise_dtypes=True), writes=[selB])
        P.op("ve", lambda e: e.tensor_scalar(out=selt[:], in0=selt[:], scalar1=0.0,
                                             scalar2=None, op0=ALU.is_equal),
             reads=[selB], writes=[selB])
        P.dma_group("sp", [
            lambda e: e.dma_start(cosb[:].rearrange("p (m f) -> p m f", m=MT),
                                  rc_h.rearrange("(m p) f -> p m f", p=128)),
            lambda e: e.dma_start(
                ssgn[:].rearrange("p (m x f) -> p m x f", m=MT, x=2)[:, :, 1, :],
                rs_h.rearrange("(m p) f -> p m f", p=128)),
            lambda e: e.dma_start(bg_sb[:], bg_h),
            lambda e: e.dma_start(sg_sb[:], sg_h),
            lambda e: e.dma_start(su_sb[:], su_h),
            lambda e: e.dma_start(sd_sb[:], sd_h),
        ], writes=[constsB], sem="dc0")
        P.op("ve", lambda e: e.tensor_scalar(
            out=ssgn[:].rearrange("p (m x f) -> p m x f", m=MT, x=2)[:, :, 0, :],
            in0=ssgn[:].rearrange("p (m x f) -> p m x f", m=MT, x=2)[:, :, 1, :],
            scalar1=-1.0, scalar2=None, op0=ALU.mult), reads=[constsB], writes=[ssgnB])

        raws = [sb(f"raws{i}", [128, 4 * DM], BF16, stack=stA) for i in range(2)]
        rawsB = P.bufs("raws", 2)
        prep_jobs = []
        jg, ju, jd = [], [], []
        for jl, (wname, wqx_h, ssb, wsp, wspB, nrows, wid) in (
                (jg, ("g", wqg_h, sg_sb, wg_sp, wgsp_B, FH, DM)),
                (ju, ("u", wqu_h, su_sb, wu_sp, wusp_B, FH, DM)),
                (jd, ("d", wqd_h, sd_sb, wd_sp, wdsp_B, DM, FH))):
            rch = (4 * DM) // wid
            rpc = rch * 128
            nch = nrows // rpc
            assert len(wspB) == nch
            for ch in range(nch):
                jl.append((wname, wqx_h, ssb, wsp, wspB[ch], ch, rch, wid))
        for a, b in zip(jg, ju):
            prep_jobs.extend([a, b])
        prep_jobs.extend(jd)

        def emit_prep(job):
            wname, wqx_h, ssb, wsp, wB, ch, rch, wid = job
            slp = ch % 2 if wname != "d" else (ch + 1) % 2
            rw = raws[slp]
            rwB = rawsB[slp]
            P.dma("ge", lambda e, wqx_h=wqx_h, ch=ch, rw=rw, wid=wid, rch=rch:
                  e.dma_start(
                rw[:, 0:rch * wid].rearrange("p (t d) -> p t d", t=rch),
                wqx_h.rearrange("(t p) d -> p t d", p=128)[:, rch * ch:rch * (ch + 1), :]),
                writes=[rwB], sem=f"dpr{slp}")
            for t in range(rch):
                rr = ch * rch + t
                P.op("ve", lambda e, t=t, rr=rr, rw=rw, ssb=ssb, wid=wid: e.tensor_scalar(
                    out=rw[:, t * wid:(t + 1) * wid], in0=rw[:, t * wid:(t + 1) * wid],
                    scalar1=128.0, scalar2=ssb[:, rr:rr + 1],
                    op0=ALU.subtract, op1=ALU.mult),
                    reads=[rwB, constsB], writes=[rwB])
            P.dma("ge", lambda e, wsp=wsp, ch=ch, rw=rw, wid=wid, rch=rch: e.dma_start(
                wsp.rearrange("(t p) d -> p t d", p=128)[:, rch * ch:rch * (ch + 1), :],
                rw[:, 0:rch * wid].rearrange("p (t d) -> p t d", t=rch)),
                reads=[rwB], writes=[wB], sem=f"dps{wname}{slp}")


        cur = {"prep": 0, "ffn": 0, "span": -1, "ii": 0}

        def emit_preps_until(n):
            while cur["prep"] < min(n, len(prep_jobs)):
                emit_prep(prep_jobs[cur["prep"]])
                cur["prep"] += 1

        emit_preps_until(6)

        xnT = sb("xnT", [128, KT * L], BF16, stack=stA, is_transpose_dest=True)
        sA2 = contextlib.ExitStack() if not cfg.flat else st
        if not cfg.flat:
            sA2.__enter__()
        selt = sb("selt", [16, 16 * 128], F32, stack=sA2)
        cosb = sb("cosb", [128, MT * 64], F32, stack=sA2)
        ssgn = sb("ssgn", [128, MT * 128], F32, stack=sA2)
        iraw = sb("iraw", [128, 128], F32, stack=sA2)
        a_tm = sb("a_tm", [128, NC * H], F32, stack=sA2)
        ainv_tm = sb("ainv_tm", [128, NC * H], F32, stack=sA2)
        r_tm = sb("r_tm", [128, NC * H], F32, stack=sA2)
        zrow = sb("zrow", [H, 128], F32, stack=sA2)
        wslot = [sb(f"ws{i}", [128, KT * 512], BF16, stack=sA2) for i in range(2)]
        wslotB = P.bufs("ws", 2)
        wgb = sb("wgb", [128, KT * H], BF16, stack=sA2)
        wgbB = P.buf("wgb")

        def load_wslot_group(w_h, nb, sl):
            P.dma_group("ge", [
                (lambda e, k=k, nb=nb, sl=sl, w_h=w_h: e.dma_start(
                    wslot[sl][:, k * 512:(k + 1) * 512],
                    w_h[k * 128:(k + 1) * 128, nb * 512:(nb + 1) * 512]))
                for k in range(KT)],
                writes=[wslotB[sl]], sem=f"dw{sl}")

        P.dma("ge", lambda e: e.dma_start(wgb[:].rearrange("p (k f) -> p k f", k=KT),
                                          wg_h.rearrange("(k p) f -> p k f", p=128)),
              writes=[wgbB], sem="dwg")
        load_wslot_group(wq_h, 0, 0)
        NXH = 2 if (L // 2) % 512 == 0 else 1
        LH = L // NXH
        xnTB = P.bufs("xnT", NXH)
        xnTB_all = list(xnTB)

        def xnT_half_tok(t0):
            return xnTB[t0 // LH]
        with contextlib.ExitStack() as s1:
            xbuf = [sb(f"xb{i}", [128, DM], F32, stack=s1) for i in range(3)]
            xnb = [sb(f"xnb{i}", [128, DM], BF16, stack=s1) for i in range(3)]
            stat = [sb(f"stat{i}", [128, 8], F32, stack=s1) for i in range(3)]
            xbufB = P.bufs("xbuf", 3)
            xnbB = P.bufs("xnb", 3)
            statB = P.bufs("stat", 3)
            for m in range(MT):
                sl = m % 3
                xb, xn, stt = xbuf[sl], xnb[sl], stat[sl]
                xbB, xnB, sttB = xbufB[sl], xnbB[sl], statB[sl]
                P.dma("se", lambda e, m=m, xb=xb: e.dma_start(xb[:], x_h[m * 128:(m + 1) * 128, :]),
                      writes=[xbB], sem=f"dx{sl}")
                P.op("ve", lambda e, xb=xb, stt=stt: e.reduce_sum(stt[:, 0:1], xb[:], axis=AX.X),
                     reads=[xbB], writes=[sttB])
                P.op("se", lambda e, xb=xb, xn=xn, stt=stt: e.activation(
                    xn[:], xb[:], AF.Square, accum_out=stt[:, 1:2]),
                    reads=[xbB], writes=[xnB, sttB])
                P.op("ve", lambda e, stt=stt: e.tensor_scalar(out=stt[:, 2:3], in0=stt[:, 0:1],
                                                              scalar1=1.0 / DM, scalar2=None,
                                                              op0=ALU.mult),
                     reads=[sttB], writes=[sttB])
                P.op("ve", lambda e, stt=stt: e.tensor_tensor(stt[:, 3:4], stt[:, 2:3], stt[:, 2:3],
                                                              ALU.mult),
                     reads=[sttB], writes=[sttB])
                P.op("ve", lambda e, stt=stt: e.tensor_scalar(out=stt[:, 4:5], in0=stt[:, 1:2],
                                                              scalar1=1.0 / DM, scalar2=stt[:, 3:4],
                                                              op0=ALU.mult, op1=ALU.subtract),
                     reads=[sttB], writes=[sttB])
                P.op("ve", lambda e, stt=stt: e.tensor_scalar(out=stt[:, 4:5], in0=stt[:, 4:5],
                                                              scalar1=LN_EPS, scalar2=None,
                                                              op0=ALU.add),
                     reads=[sttB], writes=[sttB])
                P.op("se", lambda e, stt=stt: e.activation(stt[:, 5:6], stt[:, 4:5], AF.Sqrt),
                     reads=[sttB], writes=[sttB])
                P.op("ve", lambda e, stt=stt: e.reciprocal(stt[:, 6:7], stt[:, 5:6]),
                     reads=[sttB], writes=[sttB])
                P.op("ve", lambda e, xb=xb, xn=xn, stt=stt: e.tensor_scalar(
                    out=xn[:], in0=xb[:], scalar1=stt[:, 2:3], scalar2=stt[:, 6:7],
                    op0=ALU.subtract, op1=ALU.mult),
                    reads=[xbB, sttB], writes=[xnB])
                hh = (m * 128) // LH
                P.dma("ge", lambda e, m=m, xn=xn: e.dma_start(xn_sp[m * 128:(m + 1) * 128, :], xn[:]),
                      reads=[xnB], writes=[xn_spB[m]], sem=f"dxn{hh}{sl}")
                if (m + 1) * 128 == (hh + 1) * LH:
                    P.dma_group("sp", [
                        (lambda e, k=k, hh=hh: e.dma_start(
                            xnT[:, k * L + hh * LH:k * L + (hh + 1) * LH],
                            xn_sp[hh * LH:(hh + 1) * LH, k * 128:(k + 1) * 128], transpose=True))
                        for k in range(KT)],
                        reads=[xn_spB[mm2] for mm2 in range(hh * LH // 128, (hh + 1) * LH // 128)],
                        writes=[xnTB[hh]], sem=f"dxnt{hh}")
        P.barrier()

        with contextlib.ExitStack() as s2:
            bscr = sb("bscr", [H, L], F32, stack=s2)
            aT = sb("aT", [H, L], F32, stack=s2)
            bscrB = P.buf("bscr")
            aTB = P.buf("aT")
            CPH = NC // NXH

            def emit_beta_half(hh2):
                for tb in range(LH // 512):
                    tbg = hh2 * (LH // 512) + tb
                    P.mm_group([(lambda e, k=k, tbg=tbg: e.matmul(
                        pb[0][0:H, 0:512], wgb[:, k * H:(k + 1) * H],
                        xnT[:, k * L + tbg * 512:k * L + (tbg + 1) * 512],
                        start=(k == 0), stop=(k == KT - 1))) for k in range(KT)],
                        reads=[wgbB, xnT_half_tok(tbg * 512)], writes=[pbB[0]])
                    P.op("ve", lambda e, tbg=tbg: e.tensor_scalar(
                        out=bscr[:, tbg * 512:(tbg + 1) * 512], in0=pb[0][0:H, 0:512],
                        scalar1=bg_sb[:], scalar2=None, op0=ALU.add),
                        reads=[pbB[0], constsB], writes=[bscrB])
                sl_ = slice(hh2 * LH, (hh2 + 1) * LH)
                P.op("ve", lambda e, sl_=sl_: e.tensor_scalar(
                    out=bscr[:, sl_], in0=bscr[:, sl_], scalar1=LOGIT_LO,
                    scalar2=LOGIT_HI, op0=ALU.max, op1=ALU.min),
                     reads=[bscrB], writes=[bscrB])
                P.op("ve", lambda e, sl_=sl_: e.tensor_scalar(
                    out=bscr[:, sl_], in0=bscr[:, sl_], scalar1=-1.0,
                    scalar2=None, op0=ALU.mult),
                     reads=[bscrB], writes=[bscrB])
                P.op("se", lambda e, sl_=sl_: e.activation(bscr[:, sl_], bscr[:, sl_], AF.Exp),
                     reads=[bscrB], writes=[bscrB])
                P.op("ve", lambda e, sl_=sl_: e.tensor_scalar(
                    out=bscr[:, sl_], in0=bscr[:, sl_], scalar1=1.0,
                    scalar2=None, op0=ALU.add),
                     reads=[bscrB], writes=[bscrB])
                P.op("ve", lambda e, sl_=sl_: e.reciprocal(bscr[:, sl_], bscr[:, sl_]),
                     reads=[bscrB], writes=[bscrB])
                for c in range(hh2 * CPH, (hh2 + 1) * CPH):
                    P.op("ve", lambda e, c=c: e.tensor_tensor_scan(
                        aT[:, c * C:(c + 1) * C], bscr[:, c * C:(c + 1) * C], zrow[:],
                        1.0, ALU.mult, ALU.add), reads=[bscrB, constsB], writes=[aTB])

            def emit_beta_chunk(c):
                    P.mm_group([lambda e, c=c: e.transpose(
                        pb[1][0:128, 0:H], aT[:, c * C:(c + 1) * C], identf[0:H, 0:H])],
                               reads=[aTB, identB], writes=[pbB[1]])
                    P.op("ve", lambda e, c=c: e.tensor_copy(a_tm[:, c * H:(c + 1) * H],
                                                            pb[1][0:128, 0:H]),
                         reads=[pbB[1]], writes=[beta_cB[c]])
                    P.mm_group([(lambda e, c=c, h=h: e.matmul(
                        pb[2][0:128, h:h + 1], selt[0:H, h * 128:(h + 1) * 128],
                        aT[:, c * C + C - 1:c * C + C], start=True, stop=True)) for h in range(H)],
                        reads=[selB, aTB], writes=[pbB[2]])
                    P.op("ve", lambda e, c=c: e.tensor_copy(aC_all[:, c * H:(c + 1) * H],
                                                            pb[2][0:128, 0:H]),
                         reads=[pbB[2]], writes=[beta_cB[c]])
                    P.op("ve", lambda e, c=c: e.reciprocal(ainv_tm[:, c * H:(c + 1) * H],
                                                           a_tm[:, c * H:(c + 1) * H]),
                         reads=[beta_cB[c]], writes=[beta_cB[c]])
                    P.op("ve", lambda e, c=c: e.tensor_tensor(r_tm[:, c * H:(c + 1) * H],
                                                              ainv_tm[:, c * H:(c + 1) * H],
                                                              aC_all[:, c * H:(c + 1) * H],
                                                              ALU.mult),
                         reads=[beta_cB[c]], writes=[beta_cB[c]])

            emit_beta_half(0)
            for c0 in range(NC // NXH):
                emit_beta_chunk(c0)
            swp = [sb(f"swp{i}", [128, 512], F32, stack=s2) for i in range(4)]
            tmp1 = [sb(f"tmp1{i}", [128, 512], F32, stack=s2) for i in range(4)]
            mnb_ = [sb(f"mnb{i}", [128, 512], F32, stack=s2) for i in range(4)]
            outb = [sb(f"ob{i}", [128, 512], BF16, stack=s2) for i in range(4)]
            swpB = P.bufs("swp", 4)
            tmp1B = P.bufs("t1", 4)
            mnB = P.bufs("mn", 4)
            outbB = P.bufs("ob", 4)
            projseq = [(nb, pname, w_h) for nb in range(NB)
                       for pname, w_h in (("q", wq_h), ("k", wk_h), ("v", wv_h))]
            pi = 0
            for nb, pname, w_h in projseq:
                    sl = pi % 2
                    pi += 1
                    if pi < len(projseq):
                        nnb, _, nw_h = projseq[pi]
                        load_wslot_group(nw_h, nnb, pi % 2)
                    for m in range(MT):
                        if pi == 1 and NXH > 1:
                            if m == 5:
                                emit_beta_half(1)
                            if 6 <= m and (NC // NXH) + m - 6 < NC:
                                emit_beta_chunk((NC // NXH) + m - 6)
                        pbank = m % 6
                        P.mm_group([(lambda e, k=k, m=m, sl=sl, pbank=pbank: e.matmul(
                            pb[pbank][:, 0:512], xnT[:, k * L + m * 128:k * L + (m + 1) * 128],
                            wslot[sl][:, k * 512:(k + 1) * 512],
                            start=(k == 0), stop=(k == KT - 1))) for k in range(KT)],
                            reads=[xnT_half_tok(m * 128), wslotB[sl]], writes=[pbB[pbank]])
                        osl = (m % 2) if pname == "v" else ((m % 2) if pname == "q" else 2 + m % 2)
                        ob = outb[osl]
                        obB = outbB[osl]
                        if pname == "v":
                            P.op("se", lambda e, pbank=pbank, ob=ob: e.activation(
                                ob[:], pb[pbank][:, 0:512], AF.Copy),
                                reads=[pbB[pbank]], writes=[obB])
                            P.dma("sp", lambda e, m=m, nb=nb, ob=ob: e.dma_start(
                                v_sp[m * 128:(m + 1) * 128, nb * 512:(nb + 1) * 512], ob[:]),
                                reads=[obB], writes=[v_spB[m][nb]], sem=f"dvs{m % 2}")
                            continue
                        pp = m % 4
                        swp_, t1_, mn_ = swp[pp], tmp1[pp], mnb_[pp]
                        pq4 = pb[pbank][:, 0:512].rearrange("p (h x j) -> p h x j", h=4, x=2)
                        s4 = swp_[:].rearrange("p (h x j) -> p h x j", h=4, x=2)
                        P.op("se", lambda e, pq4=pq4, s4=s4: e.activation(s4[:, :, 0, :],
                                                                         pq4[:, :, 1, :], AF.Copy),
                             reads=[pbB[pbank]], writes=[swpB[pp]])
                        P.op("se", lambda e, pq4=pq4, s4=s4: e.activation(s4[:, :, 1, :],
                                                                         pq4[:, :, 0, :], AF.Copy),
                             reads=[pbB[pbank]], writes=[swpB[pp]])
                        P.op("ve", lambda e, pq4=pq4, m=m, t1_=t1_: e.tensor_tensor(
                            t1_[:].rearrange("p (h x j) -> p h x j", h=4, x=2), pq4,
                            cosb[:, m * 64:(m + 1) * 64][:, None, None, :]
                            .broadcast_to([128, 4, 2, 64]), ALU.mult),
                            reads=[pbB[pbank], swpB[pp], constsB], writes=[tmp1B[pp]])
                        P.op("ge", lambda e, m=m, swp_=swp_: e.tensor_tensor(
                            swp_[:].rearrange("p (h j) -> p h j", h=4),
                            swp_[:].rearrange("p (h j) -> p h j", h=4),
                            ssgn[:, m * 128:(m + 1) * 128][:, None, :]
                            .broadcast_to([128, 4, 128]), ALU.mult),
                            reads=[swpB[pp], ssgnB], writes=[swpB[pp]])
                        P.op("ve", lambda e, t1_=t1_, swp_=swp_: e.tensor_tensor(
                            t1_[:], t1_[:], swp_[:], ALU.add),
                             reads=[tmp1B[pp], swpB[pp]], writes=[tmp1B[pp]])
                        P.op("ve", lambda e, mn_=mn_, t1_=t1_: e.tensor_scalar(
                            out=mn_[:], in0=t1_[:], scalar1=0.0, scalar2=None, op0=ALU.min),
                             reads=[tmp1B[pp]], writes=[mnB[pp]])
                        P.op("se", lambda e, mn_=mn_: e.activation(mn_[:], mn_[:], AF.Exp),
                             reads=[mnB[pp]], writes=[mnB[pp]])
                        P.op("ve", lambda e, t1_=t1_, mn_=mn_: e.scalar_tensor_tensor(
                            out=t1_[:], in0=t1_[:], scalar=0.0, in1=mn_[:],
                            op0=ALU.max, op1=ALU.add),
                             reads=[tmp1B[pp], mnB[pp]], writes=[tmp1B[pp]])
                        vecs = a_tm if pname == "q" else ainv_tm
                        h0 = nb * 4
                        P.op("ve", lambda e, m=m, h0=h0, ob=ob, vecs=vecs, t1_=t1_:
                             e.tensor_tensor(
                            ob[:].rearrange("p (h j) -> p h j", h=4),
                            t1_[:].rearrange("p (h j) -> p h j", h=4),
                            vecs[:, m * H + h0:m * H + h0 + 4][:, :, None]
                            .broadcast_to([128, 4, 128]), ALU.mult),
                            reads=[tmp1B[pp], beta_cB[m]], writes=[obB])
                        dst_sp = qt_sp if pname == "q" else kt_sp
                        dst_B = qt_spB if pname == "q" else kt_spB
                        P.dma("sp", lambda e, m=m, nb=nb, ob=ob, dst_sp=dst_sp: e.dma_start(
                            dst_sp[m * 128:(m + 1) * 128, nb * 512:(nb + 1) * 512], ob[:]),
                            reads=[obB], writes=[dst_B[m][nb]],
                            sem=(f"dqs{m % 2}" if pname == "q" else f"dks{m % 2}"))
                        if pname == "k":
                            ob2 = outb[m % 2]
                            ob2B = outbB[m % 2]
                            P.op("ge", lambda e, m=m, h0=h0, ob2=ob2, t1_=tmp1[m % 4]:
                                 e.tensor_tensor(
                                ob2[:].rearrange("p (h j) -> p h j", h=4),
                                t1_[:].rearrange("p (h j) -> p h j", h=4),
                                r_tm[:, m * H + h0:m * H + h0 + 4][:, :, None]
                                .broadcast_to([128, 4, 128]), ALU.mult),
                                reads=[tmp1B[m % 4], beta_cB[m]], writes=[ob2B])
                            P.dma("sp", lambda e, m=m, nb=nb, ob2=ob2: e.dma_start(
                                kh_sp[m * 128:(m + 1) * 128, nb * 512:(nb + 1) * 512], ob2[:]),
                                reads=[ob2B], writes=[kh_spB[m][nb]], sem=f"dhs{m % 2}")
        P.barrier()
        if not cfg.flat:
            sA2.close()

        with contextlib.ExitStack() as s3:
            qTg = sb("qTg", [128, HG * L], BF16, stack=s3, is_transpose_dest=True)
            kTg = sb("kTg", [128, HG * L], BF16, stack=s3, is_transpose_dest=True)
            vch = [sb(f"vch{i}", [128, HG * 128], BF16, stack=s3) for i in range(2)]
            hch = [sb(f"hch{i}", [128, HG * 128], BF16, stack=s3) for i in range(2)]
            st_f = sb("st_f", [128, H * 129], F32, stack=s3)
            st_b = sb("st_b", [128, H * 129], BF16, stack=s3)
            Am = [sb(f"Am{i}", [128, HG * 128], BF16, stack=s3) for i in range(2)]
            dsb = [sb(f"dsb{i}", [128, 8], F32, stack=s3) for i in range(2)]
            rrow = [sb(f"rrow{i}", [1, HG * 128], F32, stack=s3) for i in range(2)]
            nsb = [sb(f"nsb{i}", [128, HG * 128], F32, stack=s3) for i in range(2)]
            ytb = [sb(f"ytb{i}", [128, HG * 128], BF16, stack=s3) for i in range(2)]
            wgT = [sb(f"wgT{i}", [128, KT * FS], BF16, stack=s3, is_transpose_dest=True)
                   for i in range(2)]
            wuT = [sb(f"wuT{i}", [128, KT * FS], BF16, stack=s3, is_transpose_dest=True)
                   for i in range(2)]
            gsb = [sb(f"gsb{i}", [128, 512], BF16, stack=s3) for i in range(2)]
            gub = [sb(f"gub{i}", [128, 512], BF16, stack=s3) for i in range(2)]
            wgTB = P.bufs("wgT", 2)
            wuTB = P.bufs("wuT", 2)
            gsbB = P.bufs("gsb", 2)
            gubB = P.bufs("gub", 2)
            nft = FS // 128
            ffn_jobs = [(s, t, tb) for s in range(NS) for t in range(nft) for tb in range(TB)]
            def emit_span_loads(s):
                sl = s % 2
                need_pair = (s * FS) // 512
                emit_preps_until(2 * need_pair + 10)
                P.dma_group("sp", [
                    (lambda e, k=k, s=s, sl=sl: e.dma_start(
                        wgT[sl][:, k * FS:(k + 1) * FS],
                        wg_sp[s * FS:(s + 1) * FS, k * 128:(k + 1) * 128], transpose=True))
                    for k in range(KT)],
                    reads=[wgsp_B[i] for i in range((s * FS) // RPC_GU,
                                                    ((s + 1) * FS + RPC_GU - 1) // RPC_GU)],
                    writes=[wgTB[sl]], sem=f"dwtg{sl}")
                P.dma_group("sp", [
                    (lambda e, k=k, s=s, sl=sl: e.dma_start(
                        wuT[sl][:, k * FS:(k + 1) * FS],
                        wu_sp[s * FS:(s + 1) * FS, k * 128:(k + 1) * 128], transpose=True))
                    for k in range(KT)],
                    reads=[wusp_B[i] for i in range((s * FS) // RPC_GU,
                                                    ((s + 1) * FS + RPC_GU - 1) // RPC_GU)],
                    writes=[wuTB[sl]], sem=f"dwtu{sl}")

            def emit_ffn(n):
                for _ in range(n):
                    if cur["ffn"] >= len(ffn_jobs):
                        return
                    s, t, tb = ffn_jobs[cur["ffn"]]
                    cur["ffn"] += 1
                    while cur["span"] < min(s + 1, NS - 1):
                        emit_span_loads(cur["span"] + 1)
                        cur["span"] += 1
                    sl = s % 2
                    ii = cur["ii"]
                    cur["ii"] = 1 - ii
                    ht = s * nft + t
                    P.mm_group([(lambda e, k=k, t=t, tb=tb, sl=sl: e.matmul(
                        pb[6][:, 0:512], wgT[sl][:, k * FS + t * 128:k * FS + (t + 1) * 128],
                        xnT[:, k * L + tb * 512:k * L + (tb + 1) * 512],
                        start=(k == 0), stop=(k == KT - 1))) for k in range(KT)],
                        reads=[wgTB[sl], xnT_half_tok(tb * 512)], writes=[pbB[6]])
                    P.op("se", lambda e, ii=ii: e.activation(
                        gsb[ii][:], pb[6][:, 0:512], AF.Silu),
                        reads=[pbB[6]], writes=[gsbB[ii]])
                    P.mm_group([(lambda e, k=k, t=t, tb=tb, sl=sl: e.matmul(
                        pb[7][:, 0:512], wuT[sl][:, k * FS + t * 128:k * FS + (t + 1) * 128],
                        xnT[:, k * L + tb * 512:k * L + (tb + 1) * 512],
                        start=(k == 0), stop=(k == KT - 1))) for k in range(KT)],
                        reads=[wuTB[sl], xnT_half_tok(tb * 512)], writes=[pbB[7]])
                    P.op("ve", lambda e, ii=ii: e.scalar_tensor_tensor(
                        out=gub[ii][:], in0=pb[7][:, 0:512], scalar=1.0, in1=gsb[ii][:],
                        op0=ALU.mult, op1=ALU.mult),
                        reads=[pbB[7], gsbB[ii]], writes=[gubB[ii]])
                    P.dma("se", lambda e, ht=ht, tb=tb, ii=ii: e.dma_start(
                        gu_sp[ht, :, tb * 512:(tb + 1) * 512], gub[ii][:]),
                        reads=[gubB[ii]], writes=[gusp_B[ht][tb]], sem=f"dgu{ii}")

            qTgB, kTgB = P.buf("qTg"), P.buf("kTg")
            vchB = P.bufs("vch", 2)
            hchB = P.bufs("hch", 2)
            stfB = P.buf("stf")
            stbB = P.buf("stb")
            AmB = P.bufs("Am", 2)
            dsbB = P.bufs("dsb", 2)
            rrowB = P.bufs("rrow", 2)
            nsbB = P.bufs("nsb", 2)
            ytbB = P.bufs("ytb", 2)

            P.op("ve", lambda e: e.memset(st_f[:], 0.0), writes=[stfB])
            P.op("ve", lambda e: e.memset(st_b[:], 0.0), writes=[stbB])

            first_spans = [False]
            for g in range(NG):
                P.dma_group("sp", [
                    (lambda e, hl=hl, g=g: e.dma_start(
                        qTg[:, hl * L:(hl + 1) * L],
                        qt_sp[:, (g * HG + hl) * 128:(g * HG + hl + 1) * 128], transpose=True))
                    for hl in range(HG)],
                    reads=[qt_spB[m][g] for m in range(MT)], writes=[qTgB], sem="dqt")
                P.dma_group("sp", [
                    (lambda e, hl=hl, g=g: e.dma_start(
                        kTg[:, hl * L:(hl + 1) * L],
                        kt_sp[:, (g * HG + hl) * 128:(g * HG + hl + 1) * 128], transpose=True))
                    for hl in range(HG)],
                    reads=[kt_spB[m][g] for m in range(MT)], writes=[kTgB], sem="dkt")
                if not first_spans[0]:
                    first_spans[0] = True
                    emit_span_loads(0)
                    if NS > 1:
                        emit_span_loads(1)
                    cur["span"] = min(1, NS - 1)
                for c in range(NC):
                    sl = c % 2
                    cp = c % 2
                    bkA = 0
                    bkN = 1
                    dco = 0 if cp == 0 else 8
                    zco = 32 if cp == 0 else 40
                    Am_, dsb_, rrow_, nsb_, ytb_ = Am[cp], dsb[cp], rrow[cp], nsb[cp], ytb[cp]
                    if c % 2 == 0:
                        emit_preps_until(cur["prep"] + 1)
                    P.dma("se", lambda e, c=c, sl=sl, g=g: e.dma_start(
                        vch[sl][:], v_sp[c * 128:(c + 1) * 128, g * HG * 128:(g + 1) * HG * 128]),
                        reads=[v_spB[c][g]], writes=[vchB[sl]], sem=f"dvc{sl}")
                    P.dma("se", lambda e, c=c, sl=sl, g=g: e.dma_start(
                        hch[sl][:], kh_sp[c * 128:(c + 1) * 128, g * HG * 128:(g + 1) * HG * 128]),
                        reads=[kh_spB[c][g]], writes=[hchB[sl]], sem=f"dhc{sl}")
                    P.mm_group([(lambda e, hl=hl, c=c, bkA=bkA: e.matmul(
                        pb[bkA][:, hl * 128:(hl + 1) * 128],
                        kTg[:, hl * L + c * 128:hl * L + (c + 1) * 128],
                        qTg[:, hl * L + c * 128:hl * L + (c + 1) * 128],
                        start=True, stop=True)) for hl in range(HG)],
                        reads=[qTgB, kTgB], writes=[pbB[bkA]])
                    emit_ffn(1)
                    P.op("ve", lambda e, Am_=Am_, bkA=bkA: e.tensor_tensor(
                        Am_[:].rearrange("p (h j) -> p h j", h=HG),
                        pb[bkA][:, 0:HG * 128].rearrange("p (h j) -> p h j", h=HG),
                        maskf[:, None, :].broadcast_to([128, HG, 128]), ALU.mult),
                        reads=[pbB[bkA], maskB], writes=[AmB[cp]])
                    numer_fns = []
                    for hl in range(HG):
                        h = g * HG + hl
                        numer_fns.append(lambda e, hl=hl, h=h, c=c, bkN=bkN: e.matmul(
                            pb[bkN][:, hl * 128:(hl + 1) * 128],
                            st_b[:, h * 129:h * 129 + 128],
                            qTg[:, hl * L + c * 128:hl * L + (c + 1) * 128],
                            start=True, stop=False))
                        numer_fns.append(lambda e, hl=hl, sl=sl, Am_=Am_, bkN=bkN: e.matmul(
                            pb[bkN][:, hl * 128:(hl + 1) * 128],
                            vch[sl][:, hl * 128:(hl + 1) * 128],
                            Am_[:, hl * 128:(hl + 1) * 128], start=False, stop=True))
                    P.mm_group(numer_fns, reads=[stbB, qTgB, vchB[sl], AmB[cp]], writes=[pbB[bkN]])
                    emit_ffn(1)
                    den_fns = []
                    for hl in range(HG):
                        h = g * HG + hl
                        den_fns.append(lambda e, hl=hl, h=h, c=c, dco=dco: e.matmul(
                            pb[2][:, dco + hl:dco + hl + 1],
                            qTg[:, hl * L + c * 128:hl * L + (c + 1) * 128],
                            st_b[:, h * 129 + 128:h * 129 + 129], start=True, stop=False))
                        den_fns.append(lambda e, hl=hl, Am_=Am_, dco=dco: e.matmul(
                            pb[2][:, dco + hl:dco + hl + 1],
                            Am_[:, hl * 128:(hl + 1) * 128],
                            onesb[:, 0:1], start=False, stop=True))
                    P.mm_group(den_fns, reads=[stbB, qTgB, AmB[cp], constsB], writes=[pbB[2]])
                    emit_ffn(1)
                    P.op("ve", lambda e, dsb_=dsb_, dco=dco: e.tensor_scalar(
                        out=dsb_[:, 0:HG], in0=pb[2][:, dco:dco + HG],
                        scalar1=ATT_EPS, scalar2=None, op0=ALU.add),
                         reads=[pbB[2]], writes=[dsbB[cp]])
                    P.op("ve", lambda e, dsb_=dsb_: e.reciprocal(dsb_[:, 4:4 + HG], dsb_[:, 0:HG]),
                         reads=[dsbB[cp]], writes=[dsbB[cp]])
                    P.mm_group([(lambda e, hl=hl, dsb_=dsb_: e.transpose(
                        pb[3][0:1, hl * 128:(hl + 1) * 128], dsb_[:, 4 + hl:5 + hl], identf[:]))
                        for hl in range(HG)],
                        reads=[dsbB[cp], identB], writes=[pbB[3]])
                    P.op("ve", lambda e, rrow_=rrow_: e.tensor_copy(rrow_[:], pb[3][0:1, 0:HG * 128]),
                         reads=[pbB[3]], writes=[rrowB[cp]])
                    P.mm_group([(lambda e, hl=hl, rrow_=rrow_: e.matmul(
                        pb[4][:, hl * 128:(hl + 1) * 128], onesf[0:1, 0:128],
                        rrow_[0:1, hl * 128:(hl + 1) * 128], start=True, stop=True))
                        for hl in range(HG)],
                        reads=[rrowB[cp], constsB], writes=[pbB[4]])
                    emit_ffn(1)
                    P.op("se", lambda e, nsb_=nsb_, bkN=bkN: e.activation(
                        nsb_[:], pb[bkN][:, 0:HG * 128], AF.Copy),
                         reads=[pbB[bkN]], writes=[nsbB[cp]])
                    P.op("ve", lambda e, ytb_=ytb_, nsb_=nsb_: e.tensor_tensor(
                        ytb_[:], nsb_[:], pb[4][:, 0:HG * 128], ALU.mult),
                         reads=[nsbB[cp], pbB[4]], writes=[ytbB[cp]])
                    P.dma("se", lambda e, c=c, g=g, ytb_=ytb_: e.dma_start(
                        yt_sp[c, :, g * HG * 128:(g + 1) * HG * 128], ytb_[:]),
                        reads=[ytbB[cp]], writes=[yt_spB[c][g]], sem=f"dyt{c % 2}")
                    P.mm_group([(lambda e, hl=hl, sl=sl: e.matmul(
                        pb[5][:, hl * 128:(hl + 1) * 128], hch[sl][:, hl * 128:(hl + 1) * 128],
                        vch[sl][:, hl * 128:(hl + 1) * 128], start=True, stop=True))
                        for hl in range(HG)],
                        reads=[hchB[sl], vchB[sl]], writes=[pbB[5]])
                    P.mm_group([(lambda e, hl=hl, sl=sl, zco=zco: e.matmul(
                        pb[2][:, zco + hl:zco + hl + 1], hch[sl][:, hl * 128:(hl + 1) * 128],
                        onesb[:, 0:1], start=True, stop=True)) for hl in range(HG)],
                        reads=[hchB[sl], constsB], writes=[pbB[2]])
                    emit_ffn(1 + (c % 2))
                    h0g = g * HG
                    stS = st_f[:].rearrange("p (h q) -> p h q", q=129)[:, h0g:h0g + HG, 0:128]
                    stz = st_f[:].rearrange("p (h q) -> p h q", q=129)[:, h0g:h0g + HG, 128:129]
                    aCg = aC_all[:, c * H + h0g:c * H + h0g + HG]
                    P.op("ve", lambda e, stS=stS, aCg=aCg: e.tensor_tensor(
                        stS, stS, aCg[:, :, None].broadcast_to([128, HG, 128]), ALU.mult),
                        reads=[stfB, beta_cB[c]], writes=[stfB])
                    P.op("ve", lambda e, stS=stS: e.tensor_tensor(
                        stS, stS, pb[5][:, 0:HG * 128].rearrange("p (h j) -> p h j", h=HG),
                        ALU.add),
                        reads=[stfB, pbB[5]], writes=[stfB])
                    P.op("ve", lambda e, stz=stz, aCg=aCg: e.tensor_tensor(
                        stz, stz, aCg[:, :, None], ALU.mult),
                        reads=[stfB, beta_cB[c]], writes=[stfB])
                    P.op("ve", lambda e, stz=stz, zco=zco: e.tensor_tensor(
                        stz, stz, pb[2][:, zco:zco + HG].rearrange("p (h j) -> p h j", j=1),
                        ALU.add),
                        reads=[stfB, pbB[2]], writes=[stfB])
                    P.op("se", lambda e, g=g: e.activation(
                        st_b[:, g * HG * 129:(g + 1) * HG * 129],
                        st_f[:, g * HG * 129:(g + 1) * HG * 129], AF.Copy),
                        reads=[stfB], writes=[stbB])
            emit_ffn(len(ffn_jobs))
            emit_preps_until(len(prep_jobs))
        P.barrier()

        if not cfg.flat:
            stA.close()
        with contextlib.ExitStack() as s46:
            nhalves0 = max(1, (DM // 512) // 2)
            boW = sb("boW", [128, DM // nhalves0], F32, stack=s46)
            boWB = P.buf("boW")
            woslot = [sb(f"wos{i}", [128, KT * 512], BF16, stack=s46) for i in range(2)]
            woslotB = P.bufs("wos", 2)
            yTm = [sb(f"yTm{i}", [128, DM], BF16, stack=s46) for i in range(2)]
            yTmB = P.bufs("yTm", 2)
            s_out = [sb(f"s_out{i}", [128, 512], F32, stack=s46) for i in range(2)]
            s_outB = P.bufs("s_out", 2)
            GS = min(2, NDS)
            wdT = [sb(f"wdT{i}", [128, FKT * DDS], BF16, stack=s46, is_transpose_dest=True)
                   for i in range(GS)]
            guTm = [sb(f"guTm{i}", [128, FKT * 256], BF16, stack=s46) for i in range(2)]
            xtl = [sb(f"xtl{i}", [128, DDS], F32, stack=s46) for i in range(2)]
            stl = [sb(f"stl{i}", [128, DDS], F32, stack=s46) for i in range(2)]
            gtl = [sb(f"gtl{i}", [128, DDS], F32, stack=s46) for i in range(2)]
            wdTB = P.bufs("wdT", GS)
            guTmB = P.bufs("guTm", 2)
            xtlB = P.bufs("xtl", 2)
            stlB = P.bufs("stl", 2)
            gtlB = P.bufs("gtl", 2)
            nhalves = max(1, NB // 2)
            W2 = DM // nhalves

            def emit_wo_loads(half):
                P.dma("ge", lambda e, half=half: e.dma_start(
                    boW[:], bo_h[:, half * W2:(half + 1) * W2].broadcast_to([128, W2])),
                    writes=[boWB], sem="dc1")
                for nb in range(half * (NB // nhalves), (half + 1) * (NB // nhalves)):
                    sl = nb % 2
                    P.dma_group("ge", [
                        (lambda e, k=k, nb=nb, sl=sl: e.dma_start(
                            woslot[sl][:, k * 512:(k + 1) * 512],
                            wo_h[k * 128:(k + 1) * 128, nb * 512:(nb + 1) * 512]))
                        for k in range(KT)],
                        writes=[woslotB[sl]], sem=f"dwo{sl}")

            def emit_s4(half, m):
                msl = m % 2
                nbs = list(range(half * (NB // nhalves), (half + 1) * (NB // nhalves)))
                P.dma("se", lambda e, m=m, msl=msl: e.dma_start(yTm[msl][:], yt_sp[m, :, :]),
                      reads=[yt_spB[m][g2] for g2 in range(NG)],
                      writes=[yTmB[msl]], sem=f"dym{msl}")
                for nb in nbs:
                    sl = nb % 2
                    pbank = 6 + nb % 2
                    P.mm_group([(lambda e, k=k, msl=msl, sl=sl, pbank=pbank: e.matmul(
                        pb[pbank][:, 0:512], yTm[msl][:, k * 128:(k + 1) * 128],
                        woslot[sl][:, k * 512:(k + 1) * 512],
                        start=(k == 0), stop=(k == KT - 1))) for k in range(KT)],
                        reads=[yTmB[msl], woslotB[sl]], writes=[pbB[pbank]])
                    P.op("ve", lambda e, nb=nb, pbank=pbank, half=half: e.tensor_tensor(
                        s_out[nb % 2][:], pb[pbank][:, 0:512],
                        boW[:, (nb - half * (NB // nhalves)) * 512:
                            (nb - half * (NB // nhalves) + 1) * 512], ALU.add),
                        reads=[pbB[pbank], boWB], writes=[s_outB[nb % 2]])
                    P.dma("se", lambda e, m=m, nb=nb, sl=sl: e.dma_start(
                        s_sp[m * 128:(m + 1) * 128, nb * 512:(nb + 1) * 512],
                        s_out[nb % 2][:]),
                        reads=[s_outB[nb % 2]], writes=[s_spB[m][half]], sem=f"dss{m % 2}")

            def emit_wd_prep(gg):
                for j in range(GS):
                    n2 = gg * GS + j
                    P.dma_group("sp", [
                        (lambda e, kh=kh, n2=n2, j=j: e.dma_start(
                            wdT[j][:, kh * DDS:(kh + 1) * DDS],
                            wd_sp[n2 * DDS:(n2 + 1) * DDS, kh * 128:(kh + 1) * 128],
                            transpose=True))
                        for kh in range(FKT)],
                        reads=[wdsp_B[i] for i in range((n2 * DDS) // RPC_D,
                                                        ((n2 + 1) * DDS + RPC_D - 1) // RPC_D)],
                        writes=[wdTB[j]], sem=f"dwd{j}")

            def emit_gu_load(mp):
                msl = mp % 2
                nsplit = 4 if FKT > 16 else 1
                step = (FKT + nsplit - 1) // nsplit
                P.dma_group("se", [
                    (lambda e, mp=mp, msl=msl, k0=k0: e.dma_start(
                        guTm[msl][:, k0 * 256:min(FKT, k0 + step) * 256]
                        .rearrange("p (k f) -> p k f", f=256),
                        gu_sp.rearrange("k p f -> p k f")[:, k0:min(FKT, k0 + step),
                                                         mp * 256:(mp + 1) * 256]))
                    for k0 in range(0, FKT, step)],
                    reads=[gusp_B[k][t2] for k in range(FKT) for t2 in range(TB)],
                    writes=[guTmB[msl]], sem=f"dgm{msl}")

            def emit_s6_mp(gg, mp, last):
                msl = mp % 2
                if not last:
                    emit_gu_load((mp + 1) % (MT // 2))
                for mi in range(2):
                    m = mp * 2 + mi
                    for j in range(GS):
                        n2 = gg * GS + j
                        ii = (m * GS + j) % 2
                        pbank = 2 + (m * GS + j) % 4
                        P.dma("sp", lambda e, m=m, n2=n2, ii=ii: e.dma_start(
                            xtl[ii][:], x_h[m * 128:(m + 1) * 128, n2 * DDS:(n2 + 1) * DDS]),
                            writes=[xtlB[ii]], sem=f"dxl{ii}")
                        hf = (n2 * DDS) // max(1, DM // 2) if NB > 2 else 0
                        P.dma("se", lambda e, m=m, n2=n2, ii=ii: e.dma_start(
                            stl[ii][:], s_sp[m * 128:(m + 1) * 128, n2 * DDS:(n2 + 1) * DDS]),
                            reads=[s_spB[m][hf]], writes=[stlB[ii]], sem=f"dsl{ii}")
                        P.mm_group([(lambda e, kh=kh, mi=mi, msl=msl, j=j, pbank=pbank:
                                     e.matmul(
                            pb[pbank][:, 0:DDS],
                            guTm[msl][:, kh * 256 + mi * 128:kh * 256 + (mi + 1) * 128],
                            wdT[j][:, kh * DDS:(kh + 1) * DDS],
                            start=(kh == 0), stop=(kh == FKT - 1))) for kh in range(FKT)],
                            reads=[guTmB[msl], wdTB[j]], writes=[pbB[pbank]])
                        P.op("se", lambda e, pbank=pbank, ii=ii: e.activation(
                            gtl[ii][:], pb[pbank][:, 0:DDS], AF.Silu),
                            reads=[pbB[pbank]], writes=[gtlB[ii]])
                        P.op("ve", lambda e, ii=ii: e.tensor_tensor(
                            gtl[ii][:], gtl[ii][:], stl[ii][:], ALU.mult),
                            reads=[gtlB[ii], stlB[ii]], writes=[gtlB[ii]])
                        P.op("ge", lambda e, ii=ii: e.tensor_tensor(
                            gtl[ii][:], gtl[ii][:], xtl[ii][:], ALU.add),
                            reads=[gtlB[ii], xtlB[ii]], writes=[gtlB[ii]])
                        P.dma("sp", lambda e, m=m, n2=n2, ii=ii: e.dma_start(
                            y_h[m * 128:(m + 1) * 128, n2 * DDS:(n2 + 1) * DDS], gtl[ii][:]),
                            reads=[gtlB[ii]], writes=[], sem=f"dout{ii}")

            emit_wo_loads(0)
            for gg in range(NDS // GS):
                emit_wd_prep(gg)
                if gg == 0:
                    emit_gu_load(0)
                for m in range(MT):
                    if gg < nhalves:
                        emit_s4(gg, m)
                        if m == MT - 1 and gg + 1 < nhalves:
                            emit_wo_loads(gg + 1)
                    if m % 2 == 1:
                        emit_s6_mp(gg, (m - 1) // 2,
                                   gg == NDS // GS - 1 and m == MT - 1)
        P.final_wait("sp", "dout0")
        P.final_wait("sp", "dout1")

        with nc.Block() as blk:
            P.replay(blk)
    return nc


def make_rope_tables(L):
    inv = (1.0 / (10000.0 ** (np.arange(64, dtype=np.float64) / 64.0))).astype(np.float32)
    t = np.arange(L, dtype=np.float32)
    freqs = (t[:, None] * inv[None, :]).astype(np.float32)
    cos = np.cos(freqs.astype(np.float64)).astype(np.float32)
    sin = np.sin(freqs.astype(np.float64)).astype(np.float32)
    return cos, sin


def make_in_maps(cfg, x_full, wq, wk, wv, wg, bg, wo, bo, wqg, sg, wqu, su, wqd, sd):
    cos, sin = make_rope_tables(cfg.L)
    base = dict(
        wq=np.ascontiguousarray(wq, np.float32), wk=np.ascontiguousarray(wk, np.float32),
        wv=np.ascontiguousarray(wv, np.float32), wg=np.ascontiguousarray(wg, np.float32),
        bg=np.ascontiguousarray(np.asarray(bg, np.float32).reshape(cfg.H, 1)),
        wo=np.ascontiguousarray(wo, np.float32),
        bo=np.ascontiguousarray(np.asarray(bo, np.float32).reshape(1, cfg.DM)),
        wqg=np.ascontiguousarray(wqg, np.int32),
        sg=np.ascontiguousarray(np.asarray(sg, np.float32).reshape(cfg.FH // 128, 128).T),
        wqu=np.ascontiguousarray(wqu, np.int32),
        su=np.ascontiguousarray(np.asarray(su, np.float32).reshape(cfg.FH // 128, 128).T),
        wqd=np.ascontiguousarray(wqd, np.int32),
        sd=np.ascontiguousarray(np.asarray(sd, np.float32).reshape(cfg.DM // 128, 128).T),
        rc=cos, rs=sin,
    )
    maps = []
    for b in range(x_full.shape[0]):
        m = dict(base)
        m["x"] = np.ascontiguousarray(x_full[b], np.float32)
        maps.append(m)
    return maps


def kernel(L0_attn_x, L0_attn_Wq, L0_attn_Wk, L0_attn_Wv, L0_attn_Wg, L0_attn_bg,
           L0_attn_Wo, L0_attn_bo, L0_ffn_wq_gate, L0_ffn_s_gate, L0_ffn_wq_up,
           L0_ffn_s_up, L0_ffn_wq_down, L0_ffn_s_down):
    cfg = Cfg(L=2048, DM=2048, FH=5632)
    x = np.asarray(L0_attn_x, dtype=np.float32)
    nc = build_program(cfg)
    in_maps = make_in_maps(cfg, x, L0_attn_Wq, L0_attn_Wk, L0_attn_Wv, L0_attn_Wg,
                           L0_attn_bg, L0_attn_Wo, L0_attn_bo, L0_ffn_wq_gate,
                           L0_ffn_s_gate, L0_ffn_wq_up, L0_ffn_s_up, L0_ffn_wq_down,
                           L0_ffn_s_down)
    res = run_bass_kernel_spmd(nc, in_maps, core_ids=list(range(8)))
    out = np.stack([np.asarray(r["y"], dtype=np.float32) for r in res.results], axis=0)
    return out


# version 29
# speedup vs baseline: 1.0219x; 1.0219x over previous
import contextlib
import numpy as np
import concourse.bass as bass
import concourse.mybir as mybir
from concourse.bass_utils import run_bass_kernel_spmd

F32 = mybir.dt.float32
BF16 = mybir.dt.bfloat16
I32 = mybir.dt.int32
AF = mybir.ActivationFunctionType
ALU = mybir.AluOpType
AX = mybir.AxisListType

LOGIT_LO = float(np.log(np.float64(0.8) / 0.2))
LOGIT_HI = float(np.log(np.float64(0.9995) / 0.0005))
ATT_EPS = 1e-6
LN_EPS = 1e-5


class Cfg:
    def __init__(self, L=2048, DM=2048, FH=5632, flat=False):
        self.L, self.DM, self.FH = L, DM, FH
        self.flat = flat
        self.dbg = False
        self.H = DM // 128
        self.C = 128
        self.NC = L // 128
        self.MT = L // 128
        self.KT = DM // 128
        self.NB = DM // 512
        self.TB = L // 512
        self.HG = min(4, self.H)
        self.NG = self.H // self.HG
        self.FS = 256
        self.NS = FH // self.FS
        self.FKT = FH // 128
        self.DDS = 512
        self.NDS = DM // self.DDS


class Buf:
    __slots__ = ("name", "writer", "readers")

    def __init__(self, name):
        self.name = name
        self.writer = None
        self.readers = []


ENGINES = ("pe", "ve", "se", "ge", "sp")


class Prog:
    def __init__(self, nc, stack):
        self.nc = nc
        self.stack = stack
        self.streams = {e: [] for e in ENGINES}
        self.sems = {}
        self.counts = {}
        self.last_wait = {e: {} for e in ENGINES}
        for e in ENGINES:
            self._sem(e)

    def _sem(self, name):
        if name not in self.sems:
            self.sems[name] = self.stack.enter_context(self.nc.semaphore(name))
            self.counts[name] = 0
        return self.sems[name]

    def buf(self, name):
        return Buf(name)

    def bufs(self, name, n):
        return [Buf(f"{name}{i}") for i in range(n)]

    def _edges(self, eng, reads, writes):
        need = {}
        for b in reads:
            if b.writer:
                s, v = b.writer
                if v > need.get(s, 0):
                    need[s] = v
        for b in writes:
            if b.writer:
                s, v = b.writer
                if v > need.get(s, 0):
                    need[s] = v
            for s, v in b.readers:
                if v > need.get(s, 0):
                    need[s] = v
        lw = self.last_wait[eng]
        waits = []
        for s, v in need.items():
            if lw.get(s, 0) < v:
                lw[s] = v
                waits.append((self.sems[s], v))
        return waits

    def _commit(self, tag, reads, writes):
        for b in writes:
            b.writer = tag
            b.readers = []
        for b in reads:
            b.readers.append(tag)

    def op(self, eng, fn, reads=(), writes=(), sem=None, n_inc=1):
        sem_name = sem or eng
        self._sem(sem_name)
        waits = self._edges(eng, reads, writes)
        self.counts[sem_name] += n_inc
        val = self.counts[sem_name]
        semh = self.sems[sem_name]

        def thunk(engine, waits=waits, fn=fn, semh=semh, n_inc=n_inc):
            for sh, v in waits:
                engine.wait_ge(sh, v)
            fn(engine).then_inc(semh, n_inc)

        self.streams[eng].append(thunk)
        self._commit((sem_name, val), reads, writes)

    def mm_group(self, fns, reads=(), writes=()):
        waits = self._edges("pe", reads, writes)
        self.counts["pe"] += 1
        val = self.counts["pe"]
        semh = self.sems["pe"]

        def thunk(engine, waits=waits, fns=tuple(fns), semh=semh):
            for sh, v in waits:
                engine.wait_ge(sh, v)
            last = None
            for f in fns:
                last = f(engine)
            last.then_inc(semh, 1)

        self.streams["pe"].append(thunk)
        self._commit(("pe", val), reads, writes)

    def dma(self, queue, fn, reads=(), writes=(), sem="d"):
        self.op(queue, fn, reads=reads, writes=writes, sem=sem, n_inc=16)

    def dma_group(self, queue, fns, reads=(), writes=(), sem="d"):
        self._sem(sem)
        waits = self._edges(queue, reads, writes)
        semh = self.sems[sem]
        self.counts[sem] += 16 * len(fns)
        val = self.counts[sem]

        def thunk(engine, waits=waits, fns=tuple(fns), semh=semh):
            for sh, v in waits:
                engine.wait_ge(sh, v)
            for f in fns:
                f(engine).then_inc(semh, 16)

        self.streams[queue].append(thunk)
        self._commit((sem, val), reads, writes)

    def barrier(self):
        snap = [(s, v) for s, v in self.counts.items() if v > 0]
        for eng in ENGINES:
            lw = self.last_wait[eng]
            pend = [(self.sems[s], v) for s, v in snap if lw.get(s, 0) < v]
            for s, v in snap:
                if lw.get(s, 0) < v:
                    lw[s] = v
            if pend:
                def thunk(engine, pend=tuple(pend)):
                    for sh, v in pend:
                        engine.wait_ge(sh, v)
                self.streams[eng].append(thunk)

    def final_wait(self, eng, sem_name):
        v = self.counts[sem_name]
        semh = self.sems[sem_name]

        def thunk(engine, semh=semh, v=v):
            engine.wait_ge(semh, v)

        self.streams[eng].append(thunk)

    def replay(self, blk):
        for eng, meth in (("sp", blk.sync), ("pe", blk.tensor), ("ve", blk.vector),
                          ("se", blk.scalar), ("ge", blk.gpsimd)):
            stream = self.streams[eng]

            def body(engine, stream=stream):
                for t in stream:
                    t(engine)

            meth(body)


def build_program(cfg: Cfg):
    nc = bass.Bass("TRN2", target_bir_lowering=False, debug=False)
    L, DM, FH, H = cfg.L, cfg.DM, cfg.FH, cfg.H
    C, NC, MT, KT, NB, TB = cfg.C, cfg.NC, cfg.MT, cfg.KT, cfg.NB, cfg.TB
    HG, NG, FS, NS, FKT, DDS, NDS = cfg.HG, cfg.NG, cfg.FS, cfg.NS, cfg.FKT, cfg.DDS, cfg.NDS

    x_h = nc.dram_tensor("x", [L, DM], F32, kind="ExternalInput").ap()
    wq_h = nc.dram_tensor("wq", [DM, DM], F32, kind="ExternalInput").ap()
    wk_h = nc.dram_tensor("wk", [DM, DM], F32, kind="ExternalInput").ap()
    wv_h = nc.dram_tensor("wv", [DM, DM], F32, kind="ExternalInput").ap()
    wg_h = nc.dram_tensor("wg", [DM, H], F32, kind="ExternalInput").ap()
    bg_h = nc.dram_tensor("bg", [H, 1], F32, kind="ExternalInput").ap()
    wo_h = nc.dram_tensor("wo", [DM, DM], F32, kind="ExternalInput").ap()
    bo_h = nc.dram_tensor("bo", [1, DM], F32, kind="ExternalInput").ap()
    wqg_h = nc.dram_tensor("wqg", [FH, DM], I32, kind="ExternalInput").ap()
    sg_h = nc.dram_tensor("sg", [128, FH // 128], F32, kind="ExternalInput").ap()
    wqu_h = nc.dram_tensor("wqu", [FH, DM], I32, kind="ExternalInput").ap()
    su_h = nc.dram_tensor("su", [128, FH // 128], F32, kind="ExternalInput").ap()
    wqd_h = nc.dram_tensor("wqd", [DM, FH], I32, kind="ExternalInput").ap()
    sd_h = nc.dram_tensor("sd", [128, DM // 128], F32, kind="ExternalInput").ap()
    rc_h = nc.dram_tensor("rc", [L, 64], F32, kind="ExternalInput").ap()
    rs_h = nc.dram_tensor("rs", [L, 64], F32, kind="ExternalInput").ap()
    y_h = nc.dram_tensor("y", [L, DM], F32, kind="ExternalOutput").ap()

    dbgk = "ExternalOutput" if cfg.dbg else "Internal"
    xn_sp = nc.dram_tensor("xn_sp", [L, DM], BF16, kind=dbgk).ap()
    qt_sp = nc.dram_tensor("qt_sp", [L, DM], BF16, kind=dbgk).ap()
    kt_sp = nc.dram_tensor("kt_sp", [L, DM], BF16, kind=dbgk).ap()
    kh_sp = nc.dram_tensor("kh_sp", [L, DM], BF16, kind=dbgk).ap()
    v_sp = nc.dram_tensor("v_sp", [L, DM], BF16, kind=dbgk).ap()
    yt_sp = nc.dram_tensor("yt_sp", [NC, 128, DM], BF16, kind=dbgk).ap()
    s_sp = nc.dram_tensor("s_sp", [L, DM], F32, kind=dbgk).ap()
    wg_sp = nc.dram_tensor("wg_sp", [FH, DM], BF16, kind=dbgk).ap()
    wu_sp = nc.dram_tensor("wu_sp", [FH, DM], BF16, kind=dbgk).ap()
    wd_sp = nc.dram_tensor("wd_sp", [DM, FH], BF16, kind=dbgk).ap()
    gu_sp = nc.dram_tensor("gu_sp", [FKT, 128, L], BF16, kind=dbgk).ap()

    with contextlib.ExitStack() as st:
        P = Prog(nc, st)

        def sb(name, shape, dt, stack=st, **kw):
            if cfg.flat:
                stack = st
            return stack.enter_context(nc.sbuf_tensor(name, shape, dt, **kw))

        onesf = sb("onesf", [128, 128], F32)
        onesb = sb("onesb", [128, 8], BF16)
        sg_sb = sb("sg_sb", [128, FH // 128], F32)
        su_sb = sb("su_sb", [128, FH // 128], F32)
        sd_sb = sb("sd_sb", [128, DM // 128], F32)
        bg_sb = sb("bg_sb", [H, 1], F32)
        stA = st.enter_context(contextlib.ExitStack())
        identf = sb("identf", [128, 128], F32, stack=stA)
        maskf = sb("maskf", [128, 128], F32, stack=stA)

        aC_all = sb("aC_all", [128, NC * H], F32, stack=stA)
        pb = [st.enter_context(nc.psum_tensor(f"pb{i}", [128, 512], F32)) for i in range(8)]
        pbB = P.bufs("pb", 8)

        constsB = P.buf("consts")
        beta_cB = [Buf(f"beta_c{c}") for c in range(L // 128)]
        identB = P.buf("ident")
        maskB = P.buf("mask")
        selB = P.buf("sel")
        ssgnB = P.buf("ssgn")
        betaB = P.buf("beta")
        xn_spB = P.bufs("xn_sp", MT)
        qt_spB = [[P.buf(f"qtsp{m}_{n}") for n in range(NB)] for m in range(MT)]
        kt_spB = [[P.buf(f"ktsp{m}_{n}") for n in range(NB)] for m in range(MT)]
        kh_spB = [[P.buf(f"khsp{m}_{n}") for n in range(NB)] for m in range(MT)]
        v_spB = [[P.buf(f"vsp{m}_{n}") for n in range(NB)] for m in range(MT)]
        yt_spB = [[P.buf(f"ytsp{c}_{g}") for g in range(NG)] for c in range(NC)]
        s_spB = [[P.buf(f"ssp{m}_{hf}") for hf in range(2)] for m in range(MT)]
        RPC_GU = 128 * ((4 * DM) // DM)
        RPC_D = 128 * ((4 * DM) // FH)
        wgsp_B = P.bufs("wgsp", FH // RPC_GU)
        wusp_B = P.bufs("wusp", FH // RPC_GU)
        wdsp_B = P.bufs("wdsp", DM // RPC_D)
        gusp_B = [[P.buf(f"gusp{k}_{t}") for t in range(TB)] for k in range(FKT)]

        P.op("ge", lambda e: e.iota(iraw[:], pattern=[[1, 128]], base=0, channel_multiplier=-1,
                                    allow_small_or_imprecise_dtypes=True), writes=[identB])
        P.op("ve", lambda e: e.tensor_scalar(out=identf[:], in0=iraw[:], scalar1=0.0, scalar2=None,
                                             op0=ALU.is_equal), reads=[identB], writes=[identB])
        P.op("ge", lambda e: e.iota(maskf[:], pattern=[[1, 128]], base=0, channel_multiplier=-1,
                                    allow_small_or_imprecise_dtypes=True), writes=[maskB])
        P.op("ve", lambda e: e.tensor_scalar(out=maskf[:], in0=maskf[:], scalar1=-0.5, scalar2=None,
                                             op0=ALU.is_gt), reads=[maskB], writes=[maskB])
        P.op("ve", lambda e: e.memset(onesf[:], 1.0), writes=[constsB])
        P.op("ve", lambda e: e.memset(onesb[:], 1.0), writes=[constsB])
        P.op("ve", lambda e: e.memset(zrow[:], 0.0), writes=[constsB])
        for h in range(16):
            P.op("ge", lambda e, h=h: e.iota(selt[:, h * 128:(h + 1) * 128], pattern=[[0, 128]],
                                             base=-h, channel_multiplier=1,
                                             allow_small_or_imprecise_dtypes=True), writes=[selB])
        P.op("ve", lambda e: e.tensor_scalar(out=selt[:], in0=selt[:], scalar1=0.0,
                                             scalar2=None, op0=ALU.is_equal),
             reads=[selB], writes=[selB])
        P.dma_group("sp", [
            lambda e: e.dma_start(cosb[:].rearrange("p (m f) -> p m f", m=MT),
                                  rc_h.rearrange("(m p) f -> p m f", p=128)),
            lambda e: e.dma_start(
                ssgn[:].rearrange("p (m x f) -> p m x f", m=MT, x=2)[:, :, 1, :],
                rs_h.rearrange("(m p) f -> p m f", p=128)),
            lambda e: e.dma_start(bg_sb[:], bg_h),
            lambda e: e.dma_start(sg_sb[:], sg_h),
            lambda e: e.dma_start(su_sb[:], su_h),
            lambda e: e.dma_start(sd_sb[:], sd_h),
        ], writes=[constsB], sem="dc0")
        P.op("ve", lambda e: e.tensor_scalar(
            out=ssgn[:].rearrange("p (m x f) -> p m x f", m=MT, x=2)[:, :, 0, :],
            in0=ssgn[:].rearrange("p (m x f) -> p m x f", m=MT, x=2)[:, :, 1, :],
            scalar1=-1.0, scalar2=None, op0=ALU.mult), reads=[constsB], writes=[ssgnB])

        raws = [sb(f"raws{i}", [128, 4 * DM], BF16, stack=stA) for i in range(2)]
        rawsB = P.bufs("raws", 2)
        prep_jobs = []
        jg, ju, jd = [], [], []
        for jl, (wname, wqx_h, ssb, wsp, wspB, nrows, wid) in (
                (jg, ("g", wqg_h, sg_sb, wg_sp, wgsp_B, FH, DM)),
                (ju, ("u", wqu_h, su_sb, wu_sp, wusp_B, FH, DM)),
                (jd, ("d", wqd_h, sd_sb, wd_sp, wdsp_B, DM, FH))):
            rch = (4 * DM) // wid
            rpc = rch * 128
            nch = nrows // rpc
            assert len(wspB) == nch
            for ch in range(nch):
                jl.append((wname, wqx_h, ssb, wsp, wspB[ch], ch, rch, wid))
        for a, b in zip(jg, ju):
            prep_jobs.extend([a, b])
        prep_jobs.extend(jd)

        def emit_prep(job):
            wname, wqx_h, ssb, wsp, wB, ch, rch, wid = job
            slp = ch % 2 if wname != "d" else (ch + 1) % 2
            rw = raws[slp]
            rwB = rawsB[slp]
            P.dma("ge", lambda e, wqx_h=wqx_h, ch=ch, rw=rw, wid=wid, rch=rch:
                  e.dma_start(
                rw[:, 0:rch * wid].rearrange("p (t d) -> p t d", t=rch),
                wqx_h.rearrange("(t p) d -> p t d", p=128)[:, rch * ch:rch * (ch + 1), :]),
                writes=[rwB], sem=f"dpr{slp}")
            for t in range(rch):
                rr = ch * rch + t
                P.op("ve", lambda e, t=t, rr=rr, rw=rw, ssb=ssb, wid=wid: e.tensor_scalar(
                    out=rw[:, t * wid:(t + 1) * wid], in0=rw[:, t * wid:(t + 1) * wid],
                    scalar1=128.0, scalar2=ssb[:, rr:rr + 1],
                    op0=ALU.subtract, op1=ALU.mult),
                    reads=[rwB, constsB], writes=[rwB])
            P.dma("ge", lambda e, wsp=wsp, ch=ch, rw=rw, wid=wid, rch=rch: e.dma_start(
                wsp.rearrange("(t p) d -> p t d", p=128)[:, rch * ch:rch * (ch + 1), :],
                rw[:, 0:rch * wid].rearrange("p (t d) -> p t d", t=rch)),
                reads=[rwB], writes=[wB], sem=f"dps{wname}{slp}")


        cur = {"prep": 0, "ffn": 0, "span": -1, "ii": 0}

        def emit_preps_until(n):
            while cur["prep"] < min(n, len(prep_jobs)):
                emit_prep(prep_jobs[cur["prep"]])
                cur["prep"] += 1

        emit_preps_until(6)

        xnT = sb("xnT", [128, KT * L], BF16, stack=stA, is_transpose_dest=True)
        sA2 = contextlib.ExitStack() if not cfg.flat else st
        if not cfg.flat:
            sA2.__enter__()
        selt = sb("selt", [16, 16 * 128], F32, stack=sA2)
        cosb = sb("cosb", [128, MT * 64], F32, stack=sA2)
        ssgn = sb("ssgn", [128, MT * 128], F32, stack=sA2)
        iraw = sb("iraw", [128, 128], F32, stack=sA2)
        a_tm = sb("a_tm", [128, NC * H], F32, stack=sA2)
        ainv_tm = sb("ainv_tm", [128, NC * H], F32, stack=sA2)
        r_tm = sb("r_tm", [128, NC * H], F32, stack=sA2)
        zrow = sb("zrow", [H, 128], F32, stack=sA2)
        wslot = [sb(f"ws{i}", [128, KT * 512], BF16, stack=sA2) for i in range(2)]
        wslotB = [[Buf(f"ws{i}h{j}") for j in range(2)] for i in range(2)]
        wgb = sb("wgb", [128, KT * H], BF16, stack=sA2)
        wgbB = P.buf("wgb")

        def load_wslot_group(w_h, nb, sl):
            for j in range(2):
                P.dma_group("ge", [
                    (lambda e, k=k, nb=nb, sl=sl, w_h=w_h: e.dma_start(
                        wslot[sl][:, k * 512:(k + 1) * 512],
                        w_h[k * 128:(k + 1) * 128, nb * 512:(nb + 1) * 512]))
                    for k in range(j * KT // 2, (j + 1) * KT // 2)],
                    writes=[wslotB[sl][j]], sem=f"dw{sl}h{j}")

        P.dma("ge", lambda e: e.dma_start(wgb[:].rearrange("p (k f) -> p k f", k=KT),
                                          wg_h.rearrange("(k p) f -> p k f", p=128)),
              writes=[wgbB], sem="dwg")
        load_wslot_group(wq_h, 0, 0)
        NXH = 2 if (L // 2) % 512 == 0 else 1
        LH = L // NXH
        xnTB = P.bufs("xnT", NXH)
        xnTB_all = list(xnTB)

        def xnT_half_tok(t0):
            return xnTB[t0 // LH]
        with contextlib.ExitStack() as s1:
            xbuf = [sb(f"xb{i}", [128, DM], F32, stack=s1) for i in range(3)]
            xnb = [sb(f"xnb{i}", [128, DM], BF16, stack=s1) for i in range(3)]
            stat = [sb(f"stat{i}", [128, 8], F32, stack=s1) for i in range(3)]
            xbufB = P.bufs("xbuf", 3)
            xnbB = P.bufs("xnb", 3)
            statB = P.bufs("stat", 3)
            for m in range(MT):
                sl = m % 3
                xb, xn, stt = xbuf[sl], xnb[sl], stat[sl]
                xbB, xnB, sttB = xbufB[sl], xnbB[sl], statB[sl]
                P.dma("se", lambda e, m=m, xb=xb: e.dma_start(xb[:], x_h[m * 128:(m + 1) * 128, :]),
                      writes=[xbB], sem=f"dx{sl}")
                P.op("ve", lambda e, xb=xb, stt=stt: e.reduce_sum(stt[:, 0:1], xb[:], axis=AX.X),
                     reads=[xbB], writes=[sttB])
                P.op("se", lambda e, xb=xb, xn=xn, stt=stt: e.activation(
                    xn[:], xb[:], AF.Square, accum_out=stt[:, 1:2]),
                    reads=[xbB], writes=[xnB, sttB])
                P.op("ve", lambda e, stt=stt: e.tensor_scalar(out=stt[:, 2:3], in0=stt[:, 0:1],
                                                              scalar1=1.0 / DM, scalar2=None,
                                                              op0=ALU.mult),
                     reads=[sttB], writes=[sttB])
                P.op("ve", lambda e, stt=stt: e.tensor_tensor(stt[:, 3:4], stt[:, 2:3], stt[:, 2:3],
                                                              ALU.mult),
                     reads=[sttB], writes=[sttB])
                P.op("ve", lambda e, stt=stt: e.tensor_scalar(out=stt[:, 4:5], in0=stt[:, 1:2],
                                                              scalar1=1.0 / DM, scalar2=stt[:, 3:4],
                                                              op0=ALU.mult, op1=ALU.subtract),
                     reads=[sttB], writes=[sttB])
                P.op("ve", lambda e, stt=stt: e.tensor_scalar(out=stt[:, 4:5], in0=stt[:, 4:5],
                                                              scalar1=LN_EPS, scalar2=None,
                                                              op0=ALU.add),
                     reads=[sttB], writes=[sttB])
                P.op("se", lambda e, stt=stt: e.activation(stt[:, 5:6], stt[:, 4:5], AF.Sqrt),
                     reads=[sttB], writes=[sttB])
                P.op("ve", lambda e, stt=stt: e.reciprocal(stt[:, 6:7], stt[:, 5:6]),
                     reads=[sttB], writes=[sttB])
                P.op("ve", lambda e, xb=xb, xn=xn, stt=stt: e.tensor_scalar(
                    out=xn[:], in0=xb[:], scalar1=stt[:, 2:3], scalar2=stt[:, 6:7],
                    op0=ALU.subtract, op1=ALU.mult),
                    reads=[xbB, sttB], writes=[xnB])
                hh = (m * 128) // LH
                P.dma("ge", lambda e, m=m, xn=xn: e.dma_start(xn_sp[m * 128:(m + 1) * 128, :], xn[:]),
                      reads=[xnB], writes=[xn_spB[m]], sem=f"dxn{hh}{sl}")
                if (m + 1) * 128 == (hh + 1) * LH:
                    P.dma_group("sp", [
                        (lambda e, k=k, hh=hh: e.dma_start(
                            xnT[:, k * L + hh * LH:k * L + (hh + 1) * LH],
                            xn_sp[hh * LH:(hh + 1) * LH, k * 128:(k + 1) * 128], transpose=True))
                        for k in range(KT)],
                        reads=[xn_spB[mm2] for mm2 in range(hh * LH // 128, (hh + 1) * LH // 128)],
                        writes=[xnTB[hh]], sem=f"dxnt{hh}")
        P.barrier()

        with contextlib.ExitStack() as s2:
            bscr = sb("bscr", [H, L], F32, stack=s2)
            aT = sb("aT", [H, L], F32, stack=s2)
            bscrB = P.buf("bscr")
            aTB = P.buf("aT")
            CPH = NC // NXH

            def emit_beta_half(hh2):
                for tb in range(LH // 512):
                    tbg = hh2 * (LH // 512) + tb
                    P.mm_group([(lambda e, k=k, tbg=tbg: e.matmul(
                        pb[0][0:H, 0:512], wgb[:, k * H:(k + 1) * H],
                        xnT[:, k * L + tbg * 512:k * L + (tbg + 1) * 512],
                        start=(k == 0), stop=(k == KT - 1))) for k in range(KT)],
                        reads=[wgbB, xnT_half_tok(tbg * 512)], writes=[pbB[0]])
                    P.op("ve", lambda e, tbg=tbg: e.tensor_scalar(
                        out=bscr[:, tbg * 512:(tbg + 1) * 512], in0=pb[0][0:H, 0:512],
                        scalar1=bg_sb[:], scalar2=None, op0=ALU.add),
                        reads=[pbB[0], constsB], writes=[bscrB])
                sl_ = slice(hh2 * LH, (hh2 + 1) * LH)
                P.op("ve", lambda e, sl_=sl_: e.tensor_scalar(
                    out=bscr[:, sl_], in0=bscr[:, sl_], scalar1=LOGIT_LO,
                    scalar2=LOGIT_HI, op0=ALU.max, op1=ALU.min),
                     reads=[bscrB], writes=[bscrB])
                P.op("ve", lambda e, sl_=sl_: e.tensor_scalar(
                    out=bscr[:, sl_], in0=bscr[:, sl_], scalar1=-1.0,
                    scalar2=None, op0=ALU.mult),
                     reads=[bscrB], writes=[bscrB])
                P.op("se", lambda e, sl_=sl_: e.activation(bscr[:, sl_], bscr[:, sl_], AF.Exp),
                     reads=[bscrB], writes=[bscrB])
                P.op("ve", lambda e, sl_=sl_: e.tensor_scalar(
                    out=bscr[:, sl_], in0=bscr[:, sl_], scalar1=1.0,
                    scalar2=None, op0=ALU.add),
                     reads=[bscrB], writes=[bscrB])
                P.op("ve", lambda e, sl_=sl_: e.reciprocal(bscr[:, sl_], bscr[:, sl_]),
                     reads=[bscrB], writes=[bscrB])
                for c in range(hh2 * CPH, (hh2 + 1) * CPH):
                    P.op("ve", lambda e, c=c: e.tensor_tensor_scan(
                        aT[:, c * C:(c + 1) * C], bscr[:, c * C:(c + 1) * C], zrow[:],
                        1.0, ALU.mult, ALU.add), reads=[bscrB, constsB], writes=[aTB])

            def emit_beta_chunk(c):
                    P.mm_group([lambda e, c=c: e.transpose(
                        pb[1][0:128, 0:H], aT[:, c * C:(c + 1) * C], identf[0:H, 0:H])],
                               reads=[aTB, identB], writes=[pbB[1]])
                    P.op("ve", lambda e, c=c: e.tensor_copy(a_tm[:, c * H:(c + 1) * H],
                                                            pb[1][0:128, 0:H]),
                         reads=[pbB[1]], writes=[beta_cB[c]])
                    P.mm_group([(lambda e, c=c, h=h: e.matmul(
                        pb[2][0:128, h:h + 1], selt[0:H, h * 128:(h + 1) * 128],
                        aT[:, c * C + C - 1:c * C + C], start=True, stop=True)) for h in range(H)],
                        reads=[selB, aTB], writes=[pbB[2]])
                    P.op("ve", lambda e, c=c: e.tensor_copy(aC_all[:, c * H:(c + 1) * H],
                                                            pb[2][0:128, 0:H]),
                         reads=[pbB[2]], writes=[beta_cB[c]])
                    P.op("ve", lambda e, c=c: e.reciprocal(ainv_tm[:, c * H:(c + 1) * H],
                                                           a_tm[:, c * H:(c + 1) * H]),
                         reads=[beta_cB[c]], writes=[beta_cB[c]])
                    P.op("ve", lambda e, c=c: e.tensor_tensor(r_tm[:, c * H:(c + 1) * H],
                                                              ainv_tm[:, c * H:(c + 1) * H],
                                                              aC_all[:, c * H:(c + 1) * H],
                                                              ALU.mult),
                         reads=[beta_cB[c]], writes=[beta_cB[c]])

            emit_beta_half(0)
            for c0 in range(NC // NXH):
                emit_beta_chunk(c0)
            swp = [sb(f"swp{i}", [128, 512], F32, stack=s2) for i in range(4)]
            tmp1 = [sb(f"tmp1{i}", [128, 512], F32, stack=s2) for i in range(4)]
            mnb_ = [sb(f"mnb{i}", [128, 512], F32, stack=s2) for i in range(4)]
            outb = [sb(f"ob{i}", [128, 512], BF16, stack=s2) for i in range(4)]
            swpB = P.bufs("swp", 4)
            tmp1B = P.bufs("t1", 4)
            mnB = P.bufs("mn", 4)
            outbB = P.bufs("ob", 4)
            projseq = [(nb, pname, w_h) for nb in range(NB)
                       for pname, w_h in (("q", wq_h), ("k", wk_h), ("v", wv_h))]
            pi = 0
            for nb, pname, w_h in projseq:
                    sl = pi % 2
                    pi += 1
                    if pi < len(projseq):
                        nnb, _, nw_h = projseq[pi]
                        load_wslot_group(nw_h, nnb, pi % 2)
                    for m in range(MT):
                        if pi == 1 and NXH > 1:
                            if m == 5:
                                emit_beta_half(1)
                            if 6 <= m and (NC // NXH) + m - 6 < NC:
                                emit_beta_chunk((NC // NXH) + m - 6)
                        pbank = m % 6
                        if m == 0:
                            for j in range(2):
                                P.mm_group([(lambda e, k=k, m=m, sl=sl, pbank=pbank: e.matmul(
                                    pb[pbank][:, 0:512],
                                    xnT[:, k * L + m * 128:k * L + (m + 1) * 128],
                                    wslot[sl][:, k * 512:(k + 1) * 512],
                                    start=(k == 0), stop=(k == KT - 1)))
                                    for k in range(j * KT // 2, (j + 1) * KT // 2)],
                                    reads=[xnT_half_tok(m * 128), wslotB[sl][j]],
                                    writes=[pbB[pbank]])
                        else:
                            P.mm_group([(lambda e, k=k, m=m, sl=sl, pbank=pbank: e.matmul(
                                pb[pbank][:, 0:512], xnT[:, k * L + m * 128:k * L + (m + 1) * 128],
                                wslot[sl][:, k * 512:(k + 1) * 512],
                                start=(k == 0), stop=(k == KT - 1))) for k in range(KT)],
                                reads=[xnT_half_tok(m * 128), wslotB[sl][0], wslotB[sl][1]],
                                writes=[pbB[pbank]])
                        osl = (m % 2) if pname == "v" else ((m % 2) if pname == "q" else 2 + m % 2)
                        ob = outb[osl]
                        obB = outbB[osl]
                        if pname == "v":
                            P.op("se", lambda e, pbank=pbank, ob=ob: e.activation(
                                ob[:], pb[pbank][:, 0:512], AF.Copy),
                                reads=[pbB[pbank]], writes=[obB])
                            P.dma("sp", lambda e, m=m, nb=nb, ob=ob: e.dma_start(
                                v_sp[m * 128:(m + 1) * 128, nb * 512:(nb + 1) * 512], ob[:]),
                                reads=[obB], writes=[v_spB[m][nb]], sem=f"dvs{m % 2}")
                            continue
                        pp = m % 4
                        swp_, t1_, mn_ = swp[pp], tmp1[pp], mnb_[pp]
                        pq4 = pb[pbank][:, 0:512].rearrange("p (h x j) -> p h x j", h=4, x=2)
                        s4 = swp_[:].rearrange("p (h x j) -> p h x j", h=4, x=2)
                        P.op("se", lambda e, pq4=pq4, s4=s4: e.activation(s4[:, :, 0, :],
                                                                         pq4[:, :, 1, :], AF.Copy),
                             reads=[pbB[pbank]], writes=[swpB[pp]])
                        P.op("se", lambda e, pq4=pq4, s4=s4: e.activation(s4[:, :, 1, :],
                                                                         pq4[:, :, 0, :], AF.Copy),
                             reads=[pbB[pbank]], writes=[swpB[pp]])
                        P.op("ve", lambda e, pq4=pq4, m=m, t1_=t1_: e.tensor_tensor(
                            t1_[:].rearrange("p (h x j) -> p h x j", h=4, x=2), pq4,
                            cosb[:, m * 64:(m + 1) * 64][:, None, None, :]
                            .broadcast_to([128, 4, 2, 64]), ALU.mult),
                            reads=[pbB[pbank], swpB[pp], constsB], writes=[tmp1B[pp]])
                        P.op("ge", lambda e, m=m, swp_=swp_: e.tensor_tensor(
                            swp_[:].rearrange("p (h j) -> p h j", h=4),
                            swp_[:].rearrange("p (h j) -> p h j", h=4),
                            ssgn[:, m * 128:(m + 1) * 128][:, None, :]
                            .broadcast_to([128, 4, 128]), ALU.mult),
                            reads=[swpB[pp], ssgnB], writes=[swpB[pp]])
                        P.op("ve", lambda e, t1_=t1_, swp_=swp_: e.tensor_tensor(
                            t1_[:], t1_[:], swp_[:], ALU.add),
                             reads=[tmp1B[pp], swpB[pp]], writes=[tmp1B[pp]])
                        P.op("ve", lambda e, mn_=mn_, t1_=t1_: e.tensor_scalar(
                            out=mn_[:], in0=t1_[:], scalar1=0.0, scalar2=None, op0=ALU.min),
                             reads=[tmp1B[pp]], writes=[mnB[pp]])
                        P.op("se", lambda e, mn_=mn_: e.activation(mn_[:], mn_[:], AF.Exp),
                             reads=[mnB[pp]], writes=[mnB[pp]])
                        P.op("ve", lambda e, t1_=t1_, mn_=mn_: e.scalar_tensor_tensor(
                            out=t1_[:], in0=t1_[:], scalar=0.0, in1=mn_[:],
                            op0=ALU.max, op1=ALU.add),
                             reads=[tmp1B[pp], mnB[pp]], writes=[tmp1B[pp]])
                        vecs = a_tm if pname == "q" else ainv_tm
                        h0 = nb * 4
                        P.op("ve", lambda e, m=m, h0=h0, ob=ob, vecs=vecs, t1_=t1_:
                             e.tensor_tensor(
                            ob[:].rearrange("p (h j) -> p h j", h=4),
                            t1_[:].rearrange("p (h j) -> p h j", h=4),
                            vecs[:, m * H + h0:m * H + h0 + 4][:, :, None]
                            .broadcast_to([128, 4, 128]), ALU.mult),
                            reads=[tmp1B[pp], beta_cB[m]], writes=[obB])
                        dst_sp = qt_sp if pname == "q" else kt_sp
                        dst_B = qt_spB if pname == "q" else kt_spB
                        P.dma("sp", lambda e, m=m, nb=nb, ob=ob, dst_sp=dst_sp: e.dma_start(
                            dst_sp[m * 128:(m + 1) * 128, nb * 512:(nb + 1) * 512], ob[:]),
                            reads=[obB], writes=[dst_B[m][nb]],
                            sem=(f"dqs{m % 2}" if pname == "q" else f"dks{m % 2}"))
                        if pname == "k":
                            ob2 = outb[m % 2]
                            ob2B = outbB[m % 2]
                            P.op("ge", lambda e, m=m, h0=h0, ob2=ob2, t1_=tmp1[m % 4]:
                                 e.tensor_tensor(
                                ob2[:].rearrange("p (h j) -> p h j", h=4),
                                t1_[:].rearrange("p (h j) -> p h j", h=4),
                                r_tm[:, m * H + h0:m * H + h0 + 4][:, :, None]
                                .broadcast_to([128, 4, 128]), ALU.mult),
                                reads=[tmp1B[m % 4], beta_cB[m]], writes=[ob2B])
                            P.dma("sp", lambda e, m=m, nb=nb, ob2=ob2: e.dma_start(
                                kh_sp[m * 128:(m + 1) * 128, nb * 512:(nb + 1) * 512], ob2[:]),
                                reads=[ob2B], writes=[kh_spB[m][nb]], sem=f"dhs{m % 2}")
        P.barrier()
        if not cfg.flat:
            sA2.close()

        with contextlib.ExitStack() as s3:
            qTg = sb("qTg", [128, HG * L], BF16, stack=s3, is_transpose_dest=True)
            kTg = sb("kTg", [128, HG * L], BF16, stack=s3, is_transpose_dest=True)
            vch = [sb(f"vch{i}", [128, HG * 128], BF16, stack=s3) for i in range(2)]
            hch = [sb(f"hch{i}", [128, HG * 128], BF16, stack=s3) for i in range(2)]
            st_f = sb("st_f", [128, H * 129], F32, stack=s3)
            st_b = sb("st_b", [128, H * 129], BF16, stack=s3)
            Am = [sb(f"Am{i}", [128, HG * 128], BF16, stack=s3) for i in range(2)]
            dsb = [sb(f"dsb{i}", [128, 8], F32, stack=s3) for i in range(2)]
            rrow = [sb(f"rrow{i}", [1, HG * 128], F32, stack=s3) for i in range(2)]
            nsb = [sb(f"nsb{i}", [128, HG * 128], F32, stack=s3) for i in range(2)]
            ytb = [sb(f"ytb{i}", [128, HG * 128], BF16, stack=s3) for i in range(2)]
            wgT = [sb(f"wgT{i}", [128, KT * FS], BF16, stack=s3, is_transpose_dest=True)
                   for i in range(2)]
            wuT = [sb(f"wuT{i}", [128, KT * FS], BF16, stack=s3, is_transpose_dest=True)
                   for i in range(2)]
            gsb = [sb(f"gsb{i}", [128, 512], BF16, stack=s3) for i in range(2)]
            gub = [sb(f"gub{i}", [128, 512], BF16, stack=s3) for i in range(2)]
            wgTB = P.bufs("wgT", 2)
            wuTB = P.bufs("wuT", 2)
            gsbB = P.bufs("gsb", 2)
            gubB = P.bufs("gub", 2)
            nft = FS // 128
            ffn_jobs = [(s, t, tb) for s in range(NS) for t in range(nft) for tb in range(TB)]
            def emit_span_loads(s):
                sl = s % 2
                need_pair = (s * FS) // 512
                emit_preps_until(2 * need_pair + 10)
                P.dma_group("sp", [
                    (lambda e, k=k, s=s, sl=sl: e.dma_start(
                        wgT[sl][:, k * FS:(k + 1) * FS],
                        wg_sp[s * FS:(s + 1) * FS, k * 128:(k + 1) * 128], transpose=True))
                    for k in range(KT)],
                    reads=[wgsp_B[i] for i in range((s * FS) // RPC_GU,
                                                    ((s + 1) * FS + RPC_GU - 1) // RPC_GU)],
                    writes=[wgTB[sl]], sem=f"dwtg{sl}")
                P.dma_group("sp", [
                    (lambda e, k=k, s=s, sl=sl: e.dma_start(
                        wuT[sl][:, k * FS:(k + 1) * FS],
                        wu_sp[s * FS:(s + 1) * FS, k * 128:(k + 1) * 128], transpose=True))
                    for k in range(KT)],
                    reads=[wusp_B[i] for i in range((s * FS) // RPC_GU,
                                                    ((s + 1) * FS + RPC_GU - 1) // RPC_GU)],
                    writes=[wuTB[sl]], sem=f"dwtu{sl}")

            def emit_ffn(n):
                for _ in range(n):
                    if cur["ffn"] >= len(ffn_jobs):
                        return
                    s, t, tb = ffn_jobs[cur["ffn"]]
                    cur["ffn"] += 1
                    while cur["span"] < min(s + 1, NS - 1):
                        emit_span_loads(cur["span"] + 1)
                        cur["span"] += 1
                    sl = s % 2
                    ii = cur["ii"]
                    cur["ii"] = 1 - ii
                    ht = s * nft + t
                    P.mm_group([(lambda e, k=k, t=t, tb=tb, sl=sl: e.matmul(
                        pb[6][:, 0:512], wgT[sl][:, k * FS + t * 128:k * FS + (t + 1) * 128],
                        xnT[:, k * L + tb * 512:k * L + (tb + 1) * 512],
                        start=(k == 0), stop=(k == KT - 1))) for k in range(KT)],
                        reads=[wgTB[sl], xnT_half_tok(tb * 512)], writes=[pbB[6]])
                    P.op("se", lambda e, ii=ii: e.activation(
                        gsb[ii][:], pb[6][:, 0:512], AF.Silu),
                        reads=[pbB[6]], writes=[gsbB[ii]])
                    P.mm_group([(lambda e, k=k, t=t, tb=tb, sl=sl: e.matmul(
                        pb[7][:, 0:512], wuT[sl][:, k * FS + t * 128:k * FS + (t + 1) * 128],
                        xnT[:, k * L + tb * 512:k * L + (tb + 1) * 512],
                        start=(k == 0), stop=(k == KT - 1))) for k in range(KT)],
                        reads=[wuTB[sl], xnT_half_tok(tb * 512)], writes=[pbB[7]])
                    P.op("ve", lambda e, ii=ii: e.scalar_tensor_tensor(
                        out=gub[ii][:], in0=pb[7][:, 0:512], scalar=1.0, in1=gsb[ii][:],
                        op0=ALU.mult, op1=ALU.mult),
                        reads=[pbB[7], gsbB[ii]], writes=[gubB[ii]])
                    P.dma("se", lambda e, ht=ht, tb=tb, ii=ii: e.dma_start(
                        gu_sp[ht, :, tb * 512:(tb + 1) * 512], gub[ii][:]),
                        reads=[gubB[ii]], writes=[gusp_B[ht][tb]], sem=f"dgu{ii}")

            qTgB, kTgB = P.buf("qTg"), P.buf("kTg")
            vchB = P.bufs("vch", 2)
            hchB = P.bufs("hch", 2)
            stfB = P.buf("stf")
            stbB = P.buf("stb")
            AmB = P.bufs("Am", 2)
            dsbB = P.bufs("dsb", 2)
            rrowB = P.bufs("rrow", 2)
            nsbB = P.bufs("nsb", 2)
            ytbB = P.bufs("ytb", 2)

            P.op("ve", lambda e: e.memset(st_f[:], 0.0), writes=[stfB])
            P.op("ve", lambda e: e.memset(st_b[:], 0.0), writes=[stbB])

            first_spans = [False]
            for g in range(NG):
                P.dma_group("sp", [
                    (lambda e, hl=hl, g=g: e.dma_start(
                        qTg[:, hl * L:(hl + 1) * L],
                        qt_sp[:, (g * HG + hl) * 128:(g * HG + hl + 1) * 128], transpose=True))
                    for hl in range(HG)],
                    reads=[qt_spB[m][g] for m in range(MT)], writes=[qTgB], sem="dqt")
                P.dma_group("sp", [
                    (lambda e, hl=hl, g=g: e.dma_start(
                        kTg[:, hl * L:(hl + 1) * L],
                        kt_sp[:, (g * HG + hl) * 128:(g * HG + hl + 1) * 128], transpose=True))
                    for hl in range(HG)],
                    reads=[kt_spB[m][g] for m in range(MT)], writes=[kTgB], sem="dkt")
                if not first_spans[0]:
                    first_spans[0] = True
                    emit_span_loads(0)
                    if NS > 1:
                        emit_span_loads(1)
                    cur["span"] = min(1, NS - 1)
                for c in range(NC):
                    sl = c % 2
                    cp = c % 2
                    bkA = 0
                    bkN = 1
                    dco = 0 if cp == 0 else 8
                    zco = 32 if cp == 0 else 40
                    Am_, dsb_, rrow_, nsb_, ytb_ = Am[cp], dsb[cp], rrow[cp], nsb[cp], ytb[cp]
                    if c % 2 == 0:
                        emit_preps_until(cur["prep"] + 1)
                    P.dma("se", lambda e, c=c, sl=sl, g=g: e.dma_start(
                        vch[sl][:], v_sp[c * 128:(c + 1) * 128, g * HG * 128:(g + 1) * HG * 128]),
                        reads=[v_spB[c][g]], writes=[vchB[sl]], sem=f"dvc{sl}")
                    P.dma("se", lambda e, c=c, sl=sl, g=g: e.dma_start(
                        hch[sl][:], kh_sp[c * 128:(c + 1) * 128, g * HG * 128:(g + 1) * HG * 128]),
                        reads=[kh_spB[c][g]], writes=[hchB[sl]], sem=f"dhc{sl}")
                    P.mm_group([(lambda e, hl=hl, c=c, bkA=bkA: e.matmul(
                        pb[bkA][:, hl * 128:(hl + 1) * 128],
                        kTg[:, hl * L + c * 128:hl * L + (c + 1) * 128],
                        qTg[:, hl * L + c * 128:hl * L + (c + 1) * 128],
                        start=True, stop=True)) for hl in range(HG)],
                        reads=[qTgB, kTgB], writes=[pbB[bkA]])
                    emit_ffn(1)
                    P.op("ve", lambda e, Am_=Am_, bkA=bkA: e.tensor_tensor(
                        Am_[:].rearrange("p (h j) -> p h j", h=HG),
                        pb[bkA][:, 0:HG * 128].rearrange("p (h j) -> p h j", h=HG),
                        maskf[:, None, :].broadcast_to([128, HG, 128]), ALU.mult),
                        reads=[pbB[bkA], maskB], writes=[AmB[cp]])
                    numer_fns = []
                    for hl in range(HG):
                        h = g * HG + hl
                        numer_fns.append(lambda e, hl=hl, h=h, c=c, bkN=bkN: e.matmul(
                            pb[bkN][:, hl * 128:(hl + 1) * 128],
                            st_b[:, h * 129:h * 129 + 128],
                            qTg[:, hl * L + c * 128:hl * L + (c + 1) * 128],
                            start=True, stop=False))
                        numer_fns.append(lambda e, hl=hl, sl=sl, Am_=Am_, bkN=bkN: e.matmul(
                            pb[bkN][:, hl * 128:(hl + 1) * 128],
                            vch[sl][:, hl * 128:(hl + 1) * 128],
                            Am_[:, hl * 128:(hl + 1) * 128], start=False, stop=True))
                    P.mm_group(numer_fns, reads=[stbB, qTgB, vchB[sl], AmB[cp]], writes=[pbB[bkN]])
                    emit_ffn(1)
                    den_fns = []
                    for hl in range(HG):
                        h = g * HG + hl
                        den_fns.append(lambda e, hl=hl, h=h, c=c, dco=dco: e.matmul(
                            pb[2][:, dco + hl:dco + hl + 1],
                            qTg[:, hl * L + c * 128:hl * L + (c + 1) * 128],
                            st_b[:, h * 129 + 128:h * 129 + 129], start=True, stop=False))
                        den_fns.append(lambda e, hl=hl, Am_=Am_, dco=dco: e.matmul(
                            pb[2][:, dco + hl:dco + hl + 1],
                            Am_[:, hl * 128:(hl + 1) * 128],
                            onesb[:, 0:1], start=False, stop=True))
                    P.mm_group(den_fns, reads=[stbB, qTgB, AmB[cp], constsB], writes=[pbB[2]])
                    emit_ffn(1)
                    P.op("ve", lambda e, dsb_=dsb_, dco=dco: e.tensor_scalar(
                        out=dsb_[:, 0:HG], in0=pb[2][:, dco:dco + HG],
                        scalar1=ATT_EPS, scalar2=None, op0=ALU.add),
                         reads=[pbB[2]], writes=[dsbB[cp]])
                    P.op("ve", lambda e, dsb_=dsb_: e.reciprocal(dsb_[:, 4:4 + HG], dsb_[:, 0:HG]),
                         reads=[dsbB[cp]], writes=[dsbB[cp]])
                    P.mm_group([(lambda e, hl=hl, dsb_=dsb_: e.transpose(
                        pb[3][0:1, hl * 128:(hl + 1) * 128], dsb_[:, 4 + hl:5 + hl], identf[:]))
                        for hl in range(HG)],
                        reads=[dsbB[cp], identB], writes=[pbB[3]])
                    P.op("ve", lambda e, rrow_=rrow_: e.tensor_copy(rrow_[:], pb[3][0:1, 0:HG * 128]),
                         reads=[pbB[3]], writes=[rrowB[cp]])
                    P.mm_group([(lambda e, hl=hl, rrow_=rrow_: e.matmul(
                        pb[4][:, hl * 128:(hl + 1) * 128], onesf[0:1, 0:128],
                        rrow_[0:1, hl * 128:(hl + 1) * 128], start=True, stop=True))
                        for hl in range(HG)],
                        reads=[rrowB[cp], constsB], writes=[pbB[4]])
                    emit_ffn(1)
                    P.op("se", lambda e, nsb_=nsb_, bkN=bkN: e.activation(
                        nsb_[:], pb[bkN][:, 0:HG * 128], AF.Copy),
                         reads=[pbB[bkN]], writes=[nsbB[cp]])
                    P.op("ve", lambda e, ytb_=ytb_, nsb_=nsb_: e.tensor_tensor(
                        ytb_[:], nsb_[:], pb[4][:, 0:HG * 128], ALU.mult),
                         reads=[nsbB[cp], pbB[4]], writes=[ytbB[cp]])
                    P.dma("se", lambda e, c=c, g=g, ytb_=ytb_: e.dma_start(
                        yt_sp[c, :, g * HG * 128:(g + 1) * HG * 128], ytb_[:]),
                        reads=[ytbB[cp]], writes=[yt_spB[c][g]], sem=f"dyt{c % 2}")
                    P.mm_group([(lambda e, hl=hl, sl=sl: e.matmul(
                        pb[5][:, hl * 128:(hl + 1) * 128], hch[sl][:, hl * 128:(hl + 1) * 128],
                        vch[sl][:, hl * 128:(hl + 1) * 128], start=True, stop=True))
                        for hl in range(HG)],
                        reads=[hchB[sl], vchB[sl]], writes=[pbB[5]])
                    P.mm_group([(lambda e, hl=hl, sl=sl, zco=zco: e.matmul(
                        pb[2][:, zco + hl:zco + hl + 1], hch[sl][:, hl * 128:(hl + 1) * 128],
                        onesb[:, 0:1], start=True, stop=True)) for hl in range(HG)],
                        reads=[hchB[sl], constsB], writes=[pbB[2]])
                    emit_ffn(1 + (c % 2))
                    h0g = g * HG
                    stS = st_f[:].rearrange("p (h q) -> p h q", q=129)[:, h0g:h0g + HG, 0:128]
                    stz = st_f[:].rearrange("p (h q) -> p h q", q=129)[:, h0g:h0g + HG, 128:129]
                    aCg = aC_all[:, c * H + h0g:c * H + h0g + HG]
                    P.op("ve", lambda e, stS=stS, aCg=aCg: e.tensor_tensor(
                        stS, stS, aCg[:, :, None].broadcast_to([128, HG, 128]), ALU.mult),
                        reads=[stfB, beta_cB[c]], writes=[stfB])
                    P.op("ve", lambda e, stS=stS: e.tensor_tensor(
                        stS, stS, pb[5][:, 0:HG * 128].rearrange("p (h j) -> p h j", h=HG),
                        ALU.add),
                        reads=[stfB, pbB[5]], writes=[stfB])
                    P.op("ve", lambda e, stz=stz, aCg=aCg: e.tensor_tensor(
                        stz, stz, aCg[:, :, None], ALU.mult),
                        reads=[stfB, beta_cB[c]], writes=[stfB])
                    P.op("ve", lambda e, stz=stz, zco=zco: e.tensor_tensor(
                        stz, stz, pb[2][:, zco:zco + HG].rearrange("p (h j) -> p h j", j=1),
                        ALU.add),
                        reads=[stfB, pbB[2]], writes=[stfB])
                    P.op("se", lambda e, g=g: e.activation(
                        st_b[:, g * HG * 129:(g + 1) * HG * 129],
                        st_f[:, g * HG * 129:(g + 1) * HG * 129], AF.Copy),
                        reads=[stfB], writes=[stbB])
            emit_ffn(len(ffn_jobs))
            emit_preps_until(len(prep_jobs))
        P.barrier()

        if not cfg.flat:
            stA.close()
        with contextlib.ExitStack() as s46:
            nhalves0 = max(1, (DM // 512) // 2)
            boW = sb("boW", [128, DM // nhalves0], F32, stack=s46)
            boWB = P.buf("boW")
            woslot = [sb(f"wos{i}", [128, KT * 512], BF16, stack=s46) for i in range(2)]
            woslotB = [[Buf(f"wos{i}h{j}") for j in range(2)] for i in range(2)]
            yTm = [sb(f"yTm{i}", [128, DM], BF16, stack=s46) for i in range(2)]
            yTmB = P.bufs("yTm", 2)
            s_out = [sb(f"s_out{i}", [128, 512], F32, stack=s46) for i in range(2)]
            s_outB = P.bufs("s_out", 2)
            GS = min(2, NDS)
            wdT = [sb(f"wdT{i}", [128, FKT * DDS], BF16, stack=s46, is_transpose_dest=True)
                   for i in range(GS)]
            guTm = [sb(f"guTm{i}", [128, FKT * 256], BF16, stack=s46) for i in range(2)]
            xtl = [sb(f"xtl{i}", [128, DDS], F32, stack=s46) for i in range(2)]
            stl = [sb(f"stl{i}", [128, DDS], F32, stack=s46) for i in range(2)]
            gtl = [sb(f"gtl{i}", [128, DDS], F32, stack=s46) for i in range(2)]
            wdTB = [[Buf(f"wdT{i}h{j}") for j in range(2)] for i in range(GS)]
            guTmB = P.bufs("guTm", 2)
            xtlB = P.bufs("xtl", 2)
            stlB = P.bufs("stl", 2)
            gtlB = P.bufs("gtl", 2)
            nhalves = max(1, NB // 2)
            W2 = DM // nhalves

            def emit_wo_loads(half):
                P.dma("ge", lambda e, half=half: e.dma_start(
                    boW[:], bo_h[:, half * W2:(half + 1) * W2].broadcast_to([128, W2])),
                    writes=[boWB], sem="dc1")
                for nb in range(half * (NB // nhalves), (half + 1) * (NB // nhalves)):
                    sl = nb % 2
                    for j in range(2):
                        P.dma_group("ge", [
                            (lambda e, k=k, nb=nb, sl=sl: e.dma_start(
                                woslot[sl][:, k * 512:(k + 1) * 512],
                                wo_h[k * 128:(k + 1) * 128, nb * 512:(nb + 1) * 512]))
                            for k in range(j * KT // 2, (j + 1) * KT // 2)],
                            writes=[woslotB[sl][j]], sem=f"dwo{sl}h{j}")

            def emit_s4(half, m):
                msl = m % 2
                nbs = list(range(half * (NB // nhalves), (half + 1) * (NB // nhalves)))
                P.dma("se", lambda e, m=m, msl=msl: e.dma_start(yTm[msl][:], yt_sp[m, :, :]),
                      reads=[yt_spB[m][g2] for g2 in range(NG)],
                      writes=[yTmB[msl]], sem=f"dym{msl}")
                for nb in nbs:
                    sl = nb % 2
                    pbank = 6 + nb % 2
                    if m % MT == 0:
                        for j in range(2):
                            P.mm_group([(lambda e, k=k, msl=msl, sl=sl, pbank=pbank: e.matmul(
                                pb[pbank][:, 0:512], yTm[msl][:, k * 128:(k + 1) * 128],
                                woslot[sl][:, k * 512:(k + 1) * 512],
                                start=(k == 0), stop=(k == KT - 1)))
                                for k in range(j * KT // 2, (j + 1) * KT // 2)],
                                reads=[yTmB[msl], woslotB[sl][j]], writes=[pbB[pbank]])
                    else:
                        P.mm_group([(lambda e, k=k, msl=msl, sl=sl, pbank=pbank: e.matmul(
                            pb[pbank][:, 0:512], yTm[msl][:, k * 128:(k + 1) * 128],
                            woslot[sl][:, k * 512:(k + 1) * 512],
                            start=(k == 0), stop=(k == KT - 1))) for k in range(KT)],
                            reads=[yTmB[msl], woslotB[sl][0], woslotB[sl][1]],
                            writes=[pbB[pbank]])
                    P.op("ve", lambda e, nb=nb, pbank=pbank, half=half: e.tensor_tensor(
                        s_out[nb % 2][:], pb[pbank][:, 0:512],
                        boW[:, (nb - half * (NB // nhalves)) * 512:
                            (nb - half * (NB // nhalves) + 1) * 512], ALU.add),
                        reads=[pbB[pbank], boWB], writes=[s_outB[nb % 2]])
                    P.dma("se", lambda e, m=m, nb=nb, sl=sl: e.dma_start(
                        s_sp[m * 128:(m + 1) * 128, nb * 512:(nb + 1) * 512],
                        s_out[nb % 2][:]),
                        reads=[s_outB[nb % 2]], writes=[s_spB[m][half]], sem=f"dss{m % 2}")

            def emit_wd_prep(gg):
                for j in range(GS):
                    n2 = gg * GS + j
                    for jh in range(2):
                        P.dma_group("sp", [
                            (lambda e, kh=kh, n2=n2, j=j: e.dma_start(
                                wdT[j][:, kh * DDS:(kh + 1) * DDS],
                                wd_sp[n2 * DDS:(n2 + 1) * DDS, kh * 128:(kh + 1) * 128],
                                transpose=True))
                            for kh in range(jh * FKT // 2, (jh + 1) * (FKT // 2)
                                            if jh == 0 else FKT)],
                            reads=[wdsp_B[i] for i in range((n2 * DDS) // RPC_D,
                                                           ((n2 + 1) * DDS + RPC_D - 1) // RPC_D)],
                            writes=[wdTB[j][jh]], sem=f"dwd{j}h{jh}")

            def emit_gu_load(mp):
                msl = mp % 2
                nsplit = 4 if FKT > 16 else 1
                step = (FKT + nsplit - 1) // nsplit
                P.dma_group("se", [
                    (lambda e, mp=mp, msl=msl, k0=k0: e.dma_start(
                        guTm[msl][:, k0 * 256:min(FKT, k0 + step) * 256]
                        .rearrange("p (k f) -> p k f", f=256),
                        gu_sp.rearrange("k p f -> p k f")[:, k0:min(FKT, k0 + step),
                                                         mp * 256:(mp + 1) * 256]))
                    for k0 in range(0, FKT, step)],
                    reads=[gusp_B[k][t2] for k in range(FKT) for t2 in range(TB)],
                    writes=[guTmB[msl]], sem=f"dgm{msl}")

            def emit_s6_mp(gg, mp, last):
                msl = mp % 2
                if not last:
                    emit_gu_load((mp + 1) % (MT // 2))
                for mi in range(2):
                    m = mp * 2 + mi
                    for j in range(GS):
                        n2 = gg * GS + j
                        ii = (m * GS + j) % 2
                        pbank = 2 + (m * GS + j) % 4
                        P.dma("sp", lambda e, m=m, n2=n2, ii=ii: e.dma_start(
                            xtl[ii][:], x_h[m * 128:(m + 1) * 128, n2 * DDS:(n2 + 1) * DDS]),
                            writes=[xtlB[ii]], sem=f"dxl{ii}")
                        hf = (n2 * DDS) // max(1, DM // 2) if NB > 2 else 0
                        P.dma("se", lambda e, m=m, n2=n2, ii=ii: e.dma_start(
                            stl[ii][:], s_sp[m * 128:(m + 1) * 128, n2 * DDS:(n2 + 1) * DDS]),
                            reads=[s_spB[m][hf]], writes=[stlB[ii]], sem=f"dsl{ii}")
                        if mp == 0:
                            for jh in range(2):
                                P.mm_group([(lambda e, kh=kh, mi=mi, msl=msl, j=j, pbank=pbank:
                                             e.matmul(
                                    pb[pbank][:, 0:DDS],
                                    guTm[msl][:, kh * 256 + mi * 128:kh * 256 + (mi + 1) * 128],
                                    wdT[j][:, kh * DDS:(kh + 1) * DDS],
                                    start=(kh == 0), stop=(kh == FKT - 1)))
                                    for kh in range(jh * (FKT // 2),
                                                    (jh + 1) * (FKT // 2) if jh == 0 else FKT)],
                                    reads=[guTmB[msl], wdTB[j][jh]], writes=[pbB[pbank]])
                        else:
                            P.mm_group([(lambda e, kh=kh, mi=mi, msl=msl, j=j, pbank=pbank:
                                         e.matmul(
                                pb[pbank][:, 0:DDS],
                                guTm[msl][:, kh * 256 + mi * 128:kh * 256 + (mi + 1) * 128],
                                wdT[j][:, kh * DDS:(kh + 1) * DDS],
                                start=(kh == 0), stop=(kh == FKT - 1))) for kh in range(FKT)],
                                reads=[guTmB[msl], wdTB[j][0], wdTB[j][1]],
                                writes=[pbB[pbank]])
                        P.op("se", lambda e, pbank=pbank, ii=ii: e.activation(
                            gtl[ii][:], pb[pbank][:, 0:DDS], AF.Silu),
                            reads=[pbB[pbank]], writes=[gtlB[ii]])
                        P.op("ve", lambda e, ii=ii: e.tensor_tensor(
                            gtl[ii][:], gtl[ii][:], stl[ii][:], ALU.mult),
                            reads=[gtlB[ii], stlB[ii]], writes=[gtlB[ii]])
                        P.op("ge", lambda e, ii=ii: e.tensor_tensor(
                            gtl[ii][:], gtl[ii][:], xtl[ii][:], ALU.add),
                            reads=[gtlB[ii], xtlB[ii]], writes=[gtlB[ii]])
                        P.dma("sp", lambda e, m=m, n2=n2, ii=ii: e.dma_start(
                            y_h[m * 128:(m + 1) * 128, n2 * DDS:(n2 + 1) * DDS], gtl[ii][:]),
                            reads=[gtlB[ii]], writes=[], sem=f"dout{ii}")

            emit_wo_loads(0)
            for gg in range(NDS // GS):
                emit_wd_prep(gg)
                if gg == 0:
                    emit_gu_load(0)
                for m in range(MT):
                    if gg < nhalves:
                        emit_s4(gg, m)
                        if m == MT - 1 and gg + 1 < nhalves:
                            emit_wo_loads(gg + 1)
                    if m % 2 == 1:
                        emit_s6_mp(gg, (m - 1) // 2,
                                   gg == NDS // GS - 1 and m == MT - 1)
        P.final_wait("sp", "dout0")
        P.final_wait("sp", "dout1")

        with nc.Block() as blk:
            P.replay(blk)
    return nc


def make_rope_tables(L):
    inv = (1.0 / (10000.0 ** (np.arange(64, dtype=np.float64) / 64.0))).astype(np.float32)
    t = np.arange(L, dtype=np.float32)
    freqs = (t[:, None] * inv[None, :]).astype(np.float32)
    cos = np.cos(freqs.astype(np.float64)).astype(np.float32)
    sin = np.sin(freqs.astype(np.float64)).astype(np.float32)
    return cos, sin


def make_in_maps(cfg, x_full, wq, wk, wv, wg, bg, wo, bo, wqg, sg, wqu, su, wqd, sd):
    cos, sin = make_rope_tables(cfg.L)
    base = dict(
        wq=np.ascontiguousarray(wq, np.float32), wk=np.ascontiguousarray(wk, np.float32),
        wv=np.ascontiguousarray(wv, np.float32), wg=np.ascontiguousarray(wg, np.float32),
        bg=np.ascontiguousarray(np.asarray(bg, np.float32).reshape(cfg.H, 1)),
        wo=np.ascontiguousarray(wo, np.float32),
        bo=np.ascontiguousarray(np.asarray(bo, np.float32).reshape(1, cfg.DM)),
        wqg=np.ascontiguousarray(wqg, np.int32),
        sg=np.ascontiguousarray(np.asarray(sg, np.float32).reshape(cfg.FH // 128, 128).T),
        wqu=np.ascontiguousarray(wqu, np.int32),
        su=np.ascontiguousarray(np.asarray(su, np.float32).reshape(cfg.FH // 128, 128).T),
        wqd=np.ascontiguousarray(wqd, np.int32),
        sd=np.ascontiguousarray(np.asarray(sd, np.float32).reshape(cfg.DM // 128, 128).T),
        rc=cos, rs=sin,
    )
    maps = []
    for b in range(x_full.shape[0]):
        m = dict(base)
        m["x"] = np.ascontiguousarray(x_full[b], np.float32)
        maps.append(m)
    return maps


def kernel(L0_attn_x, L0_attn_Wq, L0_attn_Wk, L0_attn_Wv, L0_attn_Wg, L0_attn_bg,
           L0_attn_Wo, L0_attn_bo, L0_ffn_wq_gate, L0_ffn_s_gate, L0_ffn_wq_up,
           L0_ffn_s_up, L0_ffn_wq_down, L0_ffn_s_down):
    cfg = Cfg(L=2048, DM=2048, FH=5632)
    x = np.asarray(L0_attn_x, dtype=np.float32)
    nc = build_program(cfg)
    in_maps = make_in_maps(cfg, x, L0_attn_Wq, L0_attn_Wk, L0_attn_Wv, L0_attn_Wg,
                           L0_attn_bg, L0_attn_Wo, L0_attn_bo, L0_ffn_wq_gate,
                           L0_ffn_s_gate, L0_ffn_wq_up, L0_ffn_s_up, L0_ffn_wq_down,
                           L0_ffn_s_down)
    res = run_bass_kernel_spmd(nc, in_maps, core_ids=list(range(8)))
    out = np.stack([np.asarray(r["y"], dtype=np.float32) for r in res.results], axis=0)
    return out


# version 30
# speedup vs baseline: 1.0398x; 1.0174x over previous
import contextlib
import numpy as np
import concourse.bass as bass
import concourse.mybir as mybir
from concourse.bass_utils import run_bass_kernel_spmd

F32 = mybir.dt.float32
BF16 = mybir.dt.bfloat16
I32 = mybir.dt.int32
AF = mybir.ActivationFunctionType
ALU = mybir.AluOpType
AX = mybir.AxisListType

LOGIT_LO = float(np.log(np.float64(0.8) / 0.2))
LOGIT_HI = float(np.log(np.float64(0.9995) / 0.0005))
ATT_EPS = 1e-6
LN_EPS = 1e-5


class Cfg:
    def __init__(self, L=2048, DM=2048, FH=5632, flat=False):
        self.L, self.DM, self.FH = L, DM, FH
        self.flat = flat
        self.dbg = False
        self.H = DM // 128
        self.C = 128
        self.NC = L // 128
        self.MT = L // 128
        self.KT = DM // 128
        self.NB = DM // 512
        self.TB = L // 512
        self.HG = min(4, self.H)
        self.NG = self.H // self.HG
        self.FS = 256
        self.NS = FH // self.FS
        self.FKT = FH // 128
        self.DDS = 512
        self.NDS = DM // self.DDS


class Buf:
    __slots__ = ("name", "writer", "readers")

    def __init__(self, name):
        self.name = name
        self.writer = None
        self.readers = []


ENGINES = ("pe", "ve", "se", "ge", "sp")


class Prog:
    def __init__(self, nc, stack):
        self.nc = nc
        self.stack = stack
        self.streams = {e: [] for e in ENGINES}
        self.sems = {}
        self.counts = {}
        self.last_wait = {e: {} for e in ENGINES}
        for e in ENGINES:
            self._sem(e)

    def _sem(self, name):
        if name not in self.sems:
            self.sems[name] = self.stack.enter_context(self.nc.semaphore(name))
            self.counts[name] = 0
        return self.sems[name]

    def buf(self, name):
        return Buf(name)

    def bufs(self, name, n):
        return [Buf(f"{name}{i}") for i in range(n)]

    def _edges(self, eng, reads, writes):
        need = {}
        for b in reads:
            if b.writer:
                s, v = b.writer
                if v > need.get(s, 0):
                    need[s] = v
        for b in writes:
            if b.writer:
                s, v = b.writer
                if v > need.get(s, 0):
                    need[s] = v
            for s, v in b.readers:
                if v > need.get(s, 0):
                    need[s] = v
        lw = self.last_wait[eng]
        waits = []
        for s, v in need.items():
            if lw.get(s, 0) < v:
                lw[s] = v
                waits.append((self.sems[s], v))
        return waits

    def _commit(self, tag, reads, writes):
        for b in writes:
            b.writer = tag
            b.readers = []
        for b in reads:
            b.readers.append(tag)

    def op(self, eng, fn, reads=(), writes=(), sem=None, n_inc=1):
        sem_name = sem or eng
        self._sem(sem_name)
        waits = self._edges(eng, reads, writes)
        self.counts[sem_name] += n_inc
        val = self.counts[sem_name]
        semh = self.sems[sem_name]

        def thunk(engine, waits=waits, fn=fn, semh=semh, n_inc=n_inc):
            for sh, v in waits:
                engine.wait_ge(sh, v)
            fn(engine).then_inc(semh, n_inc)

        self.streams[eng].append(thunk)
        self._commit((sem_name, val), reads, writes)

    def mm_group(self, fns, reads=(), writes=()):
        waits = self._edges("pe", reads, writes)
        self.counts["pe"] += 1
        val = self.counts["pe"]
        semh = self.sems["pe"]

        def thunk(engine, waits=waits, fns=tuple(fns), semh=semh):
            for sh, v in waits:
                engine.wait_ge(sh, v)
            last = None
            for f in fns:
                last = f(engine)
            last.then_inc(semh, 1)

        self.streams["pe"].append(thunk)
        self._commit(("pe", val), reads, writes)

    def dma(self, queue, fn, reads=(), writes=(), sem="d"):
        self.op(queue, fn, reads=reads, writes=writes, sem=sem, n_inc=16)

    def dma_group(self, queue, fns, reads=(), writes=(), sem="d"):
        self._sem(sem)
        waits = self._edges(queue, reads, writes)
        semh = self.sems[sem]
        self.counts[sem] += 16 * len(fns)
        val = self.counts[sem]

        def thunk(engine, waits=waits, fns=tuple(fns), semh=semh):
            for sh, v in waits:
                engine.wait_ge(sh, v)
            for f in fns:
                f(engine).then_inc(semh, 16)

        self.streams[queue].append(thunk)
        self._commit((sem, val), reads, writes)

    def barrier(self):
        snap = [(s, v) for s, v in self.counts.items() if v > 0]
        for eng in ENGINES:
            lw = self.last_wait[eng]
            pend = [(self.sems[s], v) for s, v in snap if lw.get(s, 0) < v]
            for s, v in snap:
                if lw.get(s, 0) < v:
                    lw[s] = v
            if pend:
                def thunk(engine, pend=tuple(pend)):
                    for sh, v in pend:
                        engine.wait_ge(sh, v)
                self.streams[eng].append(thunk)

    def final_wait(self, eng, sem_name):
        v = self.counts[sem_name]
        semh = self.sems[sem_name]

        def thunk(engine, semh=semh, v=v):
            engine.wait_ge(semh, v)

        self.streams[eng].append(thunk)

    def replay(self, blk):
        for eng, meth in (("sp", blk.sync), ("pe", blk.tensor), ("ve", blk.vector),
                          ("se", blk.scalar), ("ge", blk.gpsimd)):
            stream = self.streams[eng]

            def body(engine, stream=stream):
                for t in stream:
                    t(engine)

            meth(body)


def build_program(cfg: Cfg):
    nc = bass.Bass("TRN2", target_bir_lowering=False, debug=False)
    L, DM, FH, H = cfg.L, cfg.DM, cfg.FH, cfg.H
    C, NC, MT, KT, NB, TB = cfg.C, cfg.NC, cfg.MT, cfg.KT, cfg.NB, cfg.TB
    HG, NG, FS, NS, FKT, DDS, NDS = cfg.HG, cfg.NG, cfg.FS, cfg.NS, cfg.FKT, cfg.DDS, cfg.NDS

    x_h = nc.dram_tensor("x", [L, DM], F32, kind="ExternalInput").ap()
    wq_h = nc.dram_tensor("wq", [DM, DM], F32, kind="ExternalInput").ap()
    wk_h = nc.dram_tensor("wk", [DM, DM], F32, kind="ExternalInput").ap()
    wv_h = nc.dram_tensor("wv", [DM, DM], F32, kind="ExternalInput").ap()
    wg_h = nc.dram_tensor("wg", [DM, H], F32, kind="ExternalInput").ap()
    bg_h = nc.dram_tensor("bg", [H, 1], F32, kind="ExternalInput").ap()
    wo_h = nc.dram_tensor("wo", [DM, DM], F32, kind="ExternalInput").ap()
    bo_h = nc.dram_tensor("bo", [1, DM], F32, kind="ExternalInput").ap()
    wqg_h = nc.dram_tensor("wqg", [FH, DM], I32, kind="ExternalInput").ap()
    sg_h = nc.dram_tensor("sg", [128, FH // 128], F32, kind="ExternalInput").ap()
    wqu_h = nc.dram_tensor("wqu", [FH, DM], I32, kind="ExternalInput").ap()
    su_h = nc.dram_tensor("su", [128, FH // 128], F32, kind="ExternalInput").ap()
    wqd_h = nc.dram_tensor("wqd", [DM, FH], I32, kind="ExternalInput").ap()
    sd_h = nc.dram_tensor("sd", [128, DM // 128], F32, kind="ExternalInput").ap()
    rc_h = nc.dram_tensor("rc", [L, 64], F32, kind="ExternalInput").ap()
    rs_h = nc.dram_tensor("rs", [L, 64], F32, kind="ExternalInput").ap()
    y_h = nc.dram_tensor("y", [L, DM], F32, kind="ExternalOutput").ap()

    dbgk = "ExternalOutput" if cfg.dbg else "Internal"
    xn_sp = nc.dram_tensor("xn_sp", [L, DM], BF16, kind=dbgk).ap()
    qt_sp = nc.dram_tensor("qt_sp", [L, DM], BF16, kind=dbgk).ap()
    kt_sp = nc.dram_tensor("kt_sp", [L, DM], BF16, kind=dbgk).ap()
    kh_sp = nc.dram_tensor("kh_sp", [L, DM], BF16, kind=dbgk).ap()
    v_sp = nc.dram_tensor("v_sp", [L, DM], BF16, kind=dbgk).ap()
    yt_sp = nc.dram_tensor("yt_sp", [NC, 128, DM], BF16, kind=dbgk).ap()
    s_sp = nc.dram_tensor("s_sp", [L, DM], F32, kind=dbgk).ap()
    wg_sp = nc.dram_tensor("wg_sp", [FH, DM], BF16, kind=dbgk).ap()
    wu_sp = nc.dram_tensor("wu_sp", [FH, DM], BF16, kind=dbgk).ap()
    wd_sp = nc.dram_tensor("wd_sp", [DM, FH], BF16, kind=dbgk).ap()
    gu_sp = nc.dram_tensor("gu_sp", [FKT, 128, L], BF16, kind=dbgk).ap()

    with contextlib.ExitStack() as st:
        P = Prog(nc, st)

        def sb(name, shape, dt, stack=st, **kw):
            if cfg.flat:
                stack = st
            return stack.enter_context(nc.sbuf_tensor(name, shape, dt, **kw))

        onesf = sb("onesf", [128, 128], F32)
        onesb = sb("onesb", [128, 8], BF16)
        sg_sb = sb("sg_sb", [128, FH // 128], F32)
        su_sb = sb("su_sb", [128, FH // 128], F32)
        sd_sb = sb("sd_sb", [128, DM // 128], F32)
        bg_sb = sb("bg_sb", [H, 1], F32)
        stA = st.enter_context(contextlib.ExitStack())
        identf = sb("identf", [128, 128], F32, stack=stA)
        sel4 = sb("sel4", [4, 4 * 128], F32, stack=stA)
        maskf = sb("maskf", [128, 128], F32, stack=stA)

        aC_all = sb("aC_all", [128, NC * H], F32, stack=stA)
        pb = [st.enter_context(nc.psum_tensor(f"pb{i}", [128, 512], F32)) for i in range(8)]
        pbB = P.bufs("pb", 8)

        constsB = P.buf("consts")
        beta_cB = [Buf(f"beta_c{c}") for c in range(L // 128)]
        identB = P.buf("ident")
        maskB = P.buf("mask")
        selB = P.buf("sel")
        ssgnB = P.buf("ssgn")
        betaB = P.buf("beta")
        xn_spB = P.bufs("xn_sp", MT)
        qt_spB = [[P.buf(f"qtsp{m}_{n}") for n in range(NB)] for m in range(MT)]
        kt_spB = [[P.buf(f"ktsp{m}_{n}") for n in range(NB)] for m in range(MT)]
        kh_spB = [[P.buf(f"khsp{m}_{n}") for n in range(NB)] for m in range(MT)]
        v_spB = [[P.buf(f"vsp{m}_{n}") for n in range(NB)] for m in range(MT)]
        yt_spB = [[P.buf(f"ytsp{c}_{g}") for g in range(NG)] for c in range(NC)]
        s_spB = [[P.buf(f"ssp{m}_{hf}") for hf in range(2)] for m in range(MT)]
        RPC_GU = 128 * ((4 * DM) // DM)
        RPC_D = 128 * ((4 * DM) // FH)
        wgsp_B = P.bufs("wgsp", FH // RPC_GU)
        wusp_B = P.bufs("wusp", FH // RPC_GU)
        wdsp_B = P.bufs("wdsp", DM // RPC_D)
        gusp_B = [[P.buf(f"gusp{k}_{t}") for t in range(TB)] for k in range(FKT)]

        P.op("ge", lambda e: e.iota(iraw[:], pattern=[[1, 128]], base=0, channel_multiplier=-1,
                                    allow_small_or_imprecise_dtypes=True), writes=[identB])
        P.op("ve", lambda e: e.tensor_scalar(out=identf[:], in0=iraw[:], scalar1=0.0, scalar2=None,
                                             op0=ALU.is_equal), reads=[identB], writes=[identB])
        P.op("ge", lambda e: e.iota(maskf[:], pattern=[[1, 128]], base=0, channel_multiplier=-1,
                                    allow_small_or_imprecise_dtypes=True), writes=[maskB])
        P.op("ve", lambda e: e.tensor_scalar(out=maskf[:], in0=maskf[:], scalar1=-0.5, scalar2=None,
                                             op0=ALU.is_gt), reads=[maskB], writes=[maskB])
        P.op("ve", lambda e: e.memset(onesf[:], 1.0), writes=[constsB])
        P.op("ve", lambda e: e.memset(onesb[:], 1.0), writes=[constsB])
        P.op("ve", lambda e: e.memset(zrow[:], 0.0), writes=[constsB])
        for h in range(16):
            P.op("ge", lambda e, h=h: e.iota(selt[:, h * 128:(h + 1) * 128], pattern=[[0, 128]],
                                             base=-h, channel_multiplier=1,
                                             allow_small_or_imprecise_dtypes=True), writes=[selB])
        P.op("ve", lambda e: e.tensor_scalar(out=selt[:], in0=selt[:], scalar1=0.0,
                                             scalar2=None, op0=ALU.is_equal),
             reads=[selB], writes=[selB])
        for hl in range(4):
            P.op("ge", lambda e, hl=hl: e.iota(sel4[:, hl * 128:(hl + 1) * 128],
                                               pattern=[[0, 128]], base=-hl, channel_multiplier=1,
                                               allow_small_or_imprecise_dtypes=True),
                 writes=[selB])
        P.op("ve", lambda e: e.tensor_scalar(out=sel4[:], in0=sel4[:], scalar1=0.0,
                                             scalar2=None, op0=ALU.is_equal),
             reads=[selB], writes=[selB])
        P.dma_group("sp", [
            lambda e: e.dma_start(cosb[:].rearrange("p (m f) -> p m f", m=MT),
                                  rc_h.rearrange("(m p) f -> p m f", p=128)),
            lambda e: e.dma_start(
                ssgn[:].rearrange("p (m x f) -> p m x f", m=MT, x=2)[:, :, 1, :],
                rs_h.rearrange("(m p) f -> p m f", p=128)),
            lambda e: e.dma_start(bg_sb[:], bg_h),
            lambda e: e.dma_start(sg_sb[:], sg_h),
            lambda e: e.dma_start(su_sb[:], su_h),
            lambda e: e.dma_start(sd_sb[:], sd_h),
        ], writes=[constsB], sem="dc0")
        P.op("ve", lambda e: e.tensor_scalar(
            out=ssgn[:].rearrange("p (m x f) -> p m x f", m=MT, x=2)[:, :, 0, :],
            in0=ssgn[:].rearrange("p (m x f) -> p m x f", m=MT, x=2)[:, :, 1, :],
            scalar1=-1.0, scalar2=None, op0=ALU.mult), reads=[constsB], writes=[ssgnB])

        raws = [sb(f"raws{i}", [128, 4 * DM], BF16, stack=stA) for i in range(2)]
        rawsB = P.bufs("raws", 2)
        prep_jobs = []
        jg, ju, jd = [], [], []
        for jl, (wname, wqx_h, ssb, wsp, wspB, nrows, wid) in (
                (jg, ("g", wqg_h, sg_sb, wg_sp, wgsp_B, FH, DM)),
                (ju, ("u", wqu_h, su_sb, wu_sp, wusp_B, FH, DM)),
                (jd, ("d", wqd_h, sd_sb, wd_sp, wdsp_B, DM, FH))):
            rch = (4 * DM) // wid
            rpc = rch * 128
            nch = nrows // rpc
            assert len(wspB) == nch
            for ch in range(nch):
                jl.append((wname, wqx_h, ssb, wsp, wspB[ch], ch, rch, wid))
        for a, b in zip(jg, ju):
            prep_jobs.extend([a, b])
        prep_jobs.extend(jd)

        def emit_prep(job):
            wname, wqx_h, ssb, wsp, wB, ch, rch, wid = job
            slp = ch % 2 if wname != "d" else (ch + 1) % 2
            rw = raws[slp]
            rwB = rawsB[slp]
            P.dma("ge", lambda e, wqx_h=wqx_h, ch=ch, rw=rw, wid=wid, rch=rch:
                  e.dma_start(
                rw[:, 0:rch * wid].rearrange("p (t d) -> p t d", t=rch),
                wqx_h.rearrange("(t p) d -> p t d", p=128)[:, rch * ch:rch * (ch + 1), :]),
                writes=[rwB], sem=f"dpr{slp}")
            for t in range(rch):
                rr = ch * rch + t
                P.op("ve", lambda e, t=t, rr=rr, rw=rw, ssb=ssb, wid=wid: e.tensor_scalar(
                    out=rw[:, t * wid:(t + 1) * wid], in0=rw[:, t * wid:(t + 1) * wid],
                    scalar1=128.0, scalar2=ssb[:, rr:rr + 1],
                    op0=ALU.subtract, op1=ALU.mult),
                    reads=[rwB, constsB], writes=[rwB])
            P.dma("ge", lambda e, wsp=wsp, ch=ch, rw=rw, wid=wid, rch=rch: e.dma_start(
                wsp.rearrange("(t p) d -> p t d", p=128)[:, rch * ch:rch * (ch + 1), :],
                rw[:, 0:rch * wid].rearrange("p (t d) -> p t d", t=rch)),
                reads=[rwB], writes=[wB], sem=f"dps{wname}{slp}")


        cur = {"prep": 0, "ffn": 0, "span": -1, "ii": 0}

        def emit_preps_until(n):
            while cur["prep"] < min(n, len(prep_jobs)):
                emit_prep(prep_jobs[cur["prep"]])
                cur["prep"] += 1

        emit_preps_until(6)

        xnT = sb("xnT", [128, KT * L], BF16, stack=stA, is_transpose_dest=True)
        sA2 = contextlib.ExitStack() if not cfg.flat else st
        if not cfg.flat:
            sA2.__enter__()
        selt = sb("selt", [16, 16 * 128], F32, stack=sA2)
        cosb = sb("cosb", [128, MT * 64], F32, stack=sA2)
        ssgn = sb("ssgn", [128, MT * 128], F32, stack=sA2)
        iraw = sb("iraw", [128, 128], F32, stack=sA2)
        a_tm = sb("a_tm", [128, NC * H], F32, stack=sA2)
        ainv_tm = sb("ainv_tm", [128, NC * H], F32, stack=sA2)
        r_tm = sb("r_tm", [128, NC * H], F32, stack=sA2)
        zrow = sb("zrow", [H, 128], F32, stack=sA2)
        wslot = [sb(f"ws{i}", [128, KT * 512], BF16, stack=sA2) for i in range(2)]
        wslotB = [[Buf(f"ws{i}h{j}") for j in range(2)] for i in range(2)]
        wgb = sb("wgb", [128, KT * H], BF16, stack=sA2)
        wgbB = P.buf("wgb")

        def load_wslot_group(w_h, nb, sl):
            for j in range(2):
                P.dma_group("ge", [
                    (lambda e, k=k, nb=nb, sl=sl, w_h=w_h: e.dma_start(
                        wslot[sl][:, k * 512:(k + 1) * 512],
                        w_h[k * 128:(k + 1) * 128, nb * 512:(nb + 1) * 512]))
                    for k in range(j * KT // 2, (j + 1) * KT // 2)],
                    writes=[wslotB[sl][j]], sem=f"dw{sl}h{j}")

        P.dma("ge", lambda e: e.dma_start(wgb[:].rearrange("p (k f) -> p k f", k=KT),
                                          wg_h.rearrange("(k p) f -> p k f", p=128)),
              writes=[wgbB], sem="dwg")
        load_wslot_group(wq_h, 0, 0)
        NXH = 2 if (L // 2) % 512 == 0 else 1
        LH = L // NXH
        xnTB = P.bufs("xnT", NXH)
        xnTB_all = list(xnTB)

        def xnT_half_tok(t0):
            return xnTB[t0 // LH]
        with contextlib.ExitStack() as s1:
            xbuf = [sb(f"xb{i}", [128, DM], F32, stack=s1) for i in range(3)]
            xnb = [sb(f"xnb{i}", [128, DM], BF16, stack=s1) for i in range(3)]
            stat = [sb(f"stat{i}", [128, 8], F32, stack=s1) for i in range(3)]
            xbufB = P.bufs("xbuf", 3)
            xnbB = P.bufs("xnb", 3)
            statB = P.bufs("stat", 3)
            for m in range(MT):
                sl = m % 3
                xb, xn, stt = xbuf[sl], xnb[sl], stat[sl]
                xbB, xnB, sttB = xbufB[sl], xnbB[sl], statB[sl]
                P.dma("se", lambda e, m=m, xb=xb: e.dma_start(xb[:], x_h[m * 128:(m + 1) * 128, :]),
                      writes=[xbB], sem=f"dx{sl}")
                P.op("ve", lambda e, xb=xb, stt=stt: e.reduce_sum(stt[:, 0:1], xb[:], axis=AX.X),
                     reads=[xbB], writes=[sttB])
                P.op("se", lambda e, xb=xb, xn=xn, stt=stt: e.activation(
                    xn[:], xb[:], AF.Square, accum_out=stt[:, 1:2]),
                    reads=[xbB], writes=[xnB, sttB])
                P.op("ve", lambda e, stt=stt: e.tensor_scalar(out=stt[:, 2:3], in0=stt[:, 0:1],
                                                              scalar1=1.0 / DM, scalar2=None,
                                                              op0=ALU.mult),
                     reads=[sttB], writes=[sttB])
                P.op("ve", lambda e, stt=stt: e.tensor_tensor(stt[:, 3:4], stt[:, 2:3], stt[:, 2:3],
                                                              ALU.mult),
                     reads=[sttB], writes=[sttB])
                P.op("ve", lambda e, stt=stt: e.tensor_scalar(out=stt[:, 4:5], in0=stt[:, 1:2],
                                                              scalar1=1.0 / DM, scalar2=stt[:, 3:4],
                                                              op0=ALU.mult, op1=ALU.subtract),
                     reads=[sttB], writes=[sttB])
                P.op("ve", lambda e, stt=stt: e.tensor_scalar(out=stt[:, 4:5], in0=stt[:, 4:5],
                                                              scalar1=LN_EPS, scalar2=None,
                                                              op0=ALU.add),
                     reads=[sttB], writes=[sttB])
                P.op("se", lambda e, stt=stt: e.activation(stt[:, 5:6], stt[:, 4:5], AF.Sqrt),
                     reads=[sttB], writes=[sttB])
                P.op("ve", lambda e, stt=stt: e.reciprocal(stt[:, 6:7], stt[:, 5:6]),
                     reads=[sttB], writes=[sttB])
                P.op("ve", lambda e, xb=xb, xn=xn, stt=stt: e.tensor_scalar(
                    out=xn[:], in0=xb[:], scalar1=stt[:, 2:3], scalar2=stt[:, 6:7],
                    op0=ALU.subtract, op1=ALU.mult),
                    reads=[xbB, sttB], writes=[xnB])
                hh = (m * 128) // LH
                P.dma("ge", lambda e, m=m, xn=xn: e.dma_start(xn_sp[m * 128:(m + 1) * 128, :], xn[:]),
                      reads=[xnB], writes=[xn_spB[m]], sem=f"dxn{hh}{sl}")
                if (m + 1) * 128 == (hh + 1) * LH:
                    P.dma_group("sp", [
                        (lambda e, k=k, hh=hh: e.dma_start(
                            xnT[:, k * L + hh * LH:k * L + (hh + 1) * LH],
                            xn_sp[hh * LH:(hh + 1) * LH, k * 128:(k + 1) * 128], transpose=True))
                        for k in range(KT)],
                        reads=[xn_spB[mm2] for mm2 in range(hh * LH // 128, (hh + 1) * LH // 128)],
                        writes=[xnTB[hh]], sem=f"dxnt{hh}")
        P.barrier()

        with contextlib.ExitStack() as s2:
            bscr = sb("bscr", [H, L], F32, stack=s2)
            aT = sb("aT", [H, L], F32, stack=s2)
            bscrB = P.buf("bscr")
            aTB = P.buf("aT")
            CPH = NC // NXH

            def emit_beta_half(hh2):
                for tb in range(LH // 512):
                    tbg = hh2 * (LH // 512) + tb
                    P.mm_group([(lambda e, k=k, tbg=tbg: e.matmul(
                        pb[0][0:H, 0:512], wgb[:, k * H:(k + 1) * H],
                        xnT[:, k * L + tbg * 512:k * L + (tbg + 1) * 512],
                        start=(k == 0), stop=(k == KT - 1))) for k in range(KT)],
                        reads=[wgbB, xnT_half_tok(tbg * 512)], writes=[pbB[0]])
                    P.op("ve", lambda e, tbg=tbg: e.tensor_scalar(
                        out=bscr[:, tbg * 512:(tbg + 1) * 512], in0=pb[0][0:H, 0:512],
                        scalar1=bg_sb[:], scalar2=None, op0=ALU.add),
                        reads=[pbB[0], constsB], writes=[bscrB])
                sl_ = slice(hh2 * LH, (hh2 + 1) * LH)
                P.op("ve", lambda e, sl_=sl_: e.tensor_scalar(
                    out=bscr[:, sl_], in0=bscr[:, sl_], scalar1=LOGIT_LO,
                    scalar2=LOGIT_HI, op0=ALU.max, op1=ALU.min),
                     reads=[bscrB], writes=[bscrB])
                P.op("ve", lambda e, sl_=sl_: e.tensor_scalar(
                    out=bscr[:, sl_], in0=bscr[:, sl_], scalar1=-1.0,
                    scalar2=None, op0=ALU.mult),
                     reads=[bscrB], writes=[bscrB])
                P.op("se", lambda e, sl_=sl_: e.activation(bscr[:, sl_], bscr[:, sl_], AF.Exp),
                     reads=[bscrB], writes=[bscrB])
                P.op("ve", lambda e, sl_=sl_: e.tensor_scalar(
                    out=bscr[:, sl_], in0=bscr[:, sl_], scalar1=1.0,
                    scalar2=None, op0=ALU.add),
                     reads=[bscrB], writes=[bscrB])
                P.op("ve", lambda e, sl_=sl_: e.reciprocal(bscr[:, sl_], bscr[:, sl_]),
                     reads=[bscrB], writes=[bscrB])
                for c in range(hh2 * CPH, (hh2 + 1) * CPH):
                    P.op("ve", lambda e, c=c: e.tensor_tensor_scan(
                        aT[:, c * C:(c + 1) * C], bscr[:, c * C:(c + 1) * C], zrow[:],
                        1.0, ALU.mult, ALU.add), reads=[bscrB, constsB], writes=[aTB])

            def emit_beta_chunk(c):
                    P.mm_group([lambda e, c=c: e.transpose(
                        pb[1][0:128, 0:H], aT[:, c * C:(c + 1) * C], identf[0:H, 0:H])],
                               reads=[aTB, identB], writes=[pbB[1]])
                    P.op("ve", lambda e, c=c: e.tensor_copy(a_tm[:, c * H:(c + 1) * H],
                                                            pb[1][0:128, 0:H]),
                         reads=[pbB[1]], writes=[beta_cB[c]])
                    P.op("ve", lambda e, c=c: e.reciprocal(ainv_tm[:, c * H:(c + 1) * H],
                                                           a_tm[:, c * H:(c + 1) * H]),
                         reads=[beta_cB[c]], writes=[beta_cB[c]])

            def emit_beta_tail(hh2):
                CPH2 = NC // NXH
                c0 = hh2 * CPH2
                alast = aT[:].rearrange("h (c j) -> h c j", j=C)[:, c0:c0 + CPH2, C - 1]
                for h in range(H):
                    P.mm_group([(lambda e, h=h, alast=alast, CPH2=CPH2: e.matmul(
                        pb[2][0:128, (h % 4) * CPH2:(h % 4) * CPH2 + CPH2],
                        selt[0:H, h * 128:(h + 1) * 128], alast,
                        start=True, stop=True))],
                        reads=[selB, aTB], writes=[pbB[2]])
                    P.op("ve", lambda e, h=h, c0=c0, CPH2=CPH2: e.tensor_copy(
                        aC_all[:].rearrange("p (c h2) -> p c h2", h2=H)[:, c0:c0 + CPH2, h],
                        pb[2][0:128, (h % 4) * CPH2:(h % 4) * CPH2 + CPH2]),
                        reads=[pbB[2]] + [beta_cB[c] for c in range(c0, c0 + CPH2)],
                        writes=[beta_cB[c] for c in range(c0, c0 + CPH2)])
                for c in range(c0, c0 + CPH2):
                    P.op("ve", lambda e, c=c: e.tensor_tensor(r_tm[:, c * H:(c + 1) * H],
                                                              ainv_tm[:, c * H:(c + 1) * H],
                                                              aC_all[:, c * H:(c + 1) * H],
                                                              ALU.mult),
                         reads=[beta_cB[c]], writes=[beta_cB[c]])

            emit_beta_half(0)
            for c0 in range(NC // NXH):
                emit_beta_chunk(c0)
            emit_beta_tail(0)
            swp = [sb(f"swp{i}", [128, 512], F32, stack=s2) for i in range(4)]
            tmp1 = [sb(f"tmp1{i}", [128, 512], F32, stack=s2) for i in range(4)]
            mnb_ = [sb(f"mnb{i}", [128, 512], F32, stack=s2) for i in range(4)]
            outb = [sb(f"ob{i}", [128, 512], BF16, stack=s2) for i in range(4)]
            swpB = P.bufs("swp", 4)
            tmp1B = P.bufs("t1", 4)
            mnB = P.bufs("mn", 4)
            outbB = P.bufs("ob", 4)
            projseq = [(nb, pname, w_h) for nb in range(NB)
                       for pname, w_h in (("q", wq_h), ("k", wk_h), ("v", wv_h))]
            pi = 0
            for nb, pname, w_h in projseq:
                    sl = pi % 2
                    pi += 1
                    if pi < len(projseq):
                        nnb, _, nw_h = projseq[pi]
                        load_wslot_group(nw_h, nnb, pi % 2)
                    for m in range(MT):
                        if pi == 1 and NXH > 1 and m == 5:
                            emit_beta_half(1)
                            for c1 in range(NC // NXH, NC):
                                emit_beta_chunk(c1)
                            emit_beta_tail(1)
                        pbank = m % 6
                        if m == 0:
                            for j in range(2):
                                P.mm_group([(lambda e, k=k, m=m, sl=sl, pbank=pbank: e.matmul(
                                    pb[pbank][:, 0:512],
                                    xnT[:, k * L + m * 128:k * L + (m + 1) * 128],
                                    wslot[sl][:, k * 512:(k + 1) * 512],
                                    start=(k == 0), stop=(k == KT - 1)))
                                    for k in range(j * KT // 2, (j + 1) * KT // 2)],
                                    reads=[xnT_half_tok(m * 128), wslotB[sl][j]],
                                    writes=[pbB[pbank]])
                        else:
                            P.mm_group([(lambda e, k=k, m=m, sl=sl, pbank=pbank: e.matmul(
                                pb[pbank][:, 0:512], xnT[:, k * L + m * 128:k * L + (m + 1) * 128],
                                wslot[sl][:, k * 512:(k + 1) * 512],
                                start=(k == 0), stop=(k == KT - 1))) for k in range(KT)],
                                reads=[xnT_half_tok(m * 128), wslotB[sl][0], wslotB[sl][1]],
                                writes=[pbB[pbank]])
                        osl = (m % 2) if pname == "v" else ((m % 2) if pname == "q" else 2 + m % 2)
                        ob = outb[osl]
                        obB = outbB[osl]
                        if pname == "v":
                            P.op("se", lambda e, pbank=pbank, ob=ob: e.activation(
                                ob[:], pb[pbank][:, 0:512], AF.Copy),
                                reads=[pbB[pbank]], writes=[obB])
                            P.dma("sp", lambda e, m=m, nb=nb, ob=ob: e.dma_start(
                                v_sp[m * 128:(m + 1) * 128, nb * 512:(nb + 1) * 512], ob[:]),
                                reads=[obB], writes=[v_spB[m][nb]], sem=f"dvs{m % 2}")
                            continue
                        pp = m % 4
                        swp_, t1_, mn_ = swp[pp], tmp1[pp], mnb_[pp]
                        pq4 = pb[pbank][:, 0:512].rearrange("p (h x j) -> p h x j", h=4, x=2)
                        s4 = swp_[:].rearrange("p (h x j) -> p h x j", h=4, x=2)
                        P.op("se", lambda e, pq4=pq4, s4=s4: e.activation(s4[:, :, 0, :],
                                                                         pq4[:, :, 1, :], AF.Copy),
                             reads=[pbB[pbank]], writes=[swpB[pp]])
                        P.op("se", lambda e, pq4=pq4, s4=s4: e.activation(s4[:, :, 1, :],
                                                                         pq4[:, :, 0, :], AF.Copy),
                             reads=[pbB[pbank]], writes=[swpB[pp]])
                        P.op("ve", lambda e, pq4=pq4, m=m, t1_=t1_: e.tensor_tensor(
                            t1_[:].rearrange("p (h x j) -> p h x j", h=4, x=2), pq4,
                            cosb[:, m * 64:(m + 1) * 64][:, None, None, :]
                            .broadcast_to([128, 4, 2, 64]), ALU.mult),
                            reads=[pbB[pbank], swpB[pp], constsB], writes=[tmp1B[pp]])
                        P.op("ge", lambda e, m=m, swp_=swp_: e.tensor_tensor(
                            swp_[:].rearrange("p (h j) -> p h j", h=4),
                            swp_[:].rearrange("p (h j) -> p h j", h=4),
                            ssgn[:, m * 128:(m + 1) * 128][:, None, :]
                            .broadcast_to([128, 4, 128]), ALU.mult),
                            reads=[swpB[pp], ssgnB], writes=[swpB[pp]])
                        P.op("ve", lambda e, t1_=t1_, swp_=swp_: e.tensor_tensor(
                            t1_[:], t1_[:], swp_[:], ALU.add),
                             reads=[tmp1B[pp], swpB[pp]], writes=[tmp1B[pp]])
                        P.op("ve", lambda e, mn_=mn_, t1_=t1_: e.tensor_scalar(
                            out=mn_[:], in0=t1_[:], scalar1=0.0, scalar2=None, op0=ALU.min),
                             reads=[tmp1B[pp]], writes=[mnB[pp]])
                        P.op("se", lambda e, mn_=mn_: e.activation(mn_[:], mn_[:], AF.Exp),
                             reads=[mnB[pp]], writes=[mnB[pp]])
                        P.op("ve", lambda e, t1_=t1_, mn_=mn_: e.scalar_tensor_tensor(
                            out=t1_[:], in0=t1_[:], scalar=0.0, in1=mn_[:],
                            op0=ALU.max, op1=ALU.add),
                             reads=[tmp1B[pp], mnB[pp]], writes=[tmp1B[pp]])
                        vecs = a_tm if pname == "q" else ainv_tm
                        h0 = nb * 4
                        P.op("ve", lambda e, m=m, h0=h0, ob=ob, vecs=vecs, t1_=t1_:
                             e.tensor_tensor(
                            ob[:].rearrange("p (h j) -> p h j", h=4),
                            t1_[:].rearrange("p (h j) -> p h j", h=4),
                            vecs[:, m * H + h0:m * H + h0 + 4][:, :, None]
                            .broadcast_to([128, 4, 128]), ALU.mult),
                            reads=[tmp1B[pp], beta_cB[m]], writes=[obB])
                        dst_sp = qt_sp if pname == "q" else kt_sp
                        dst_B = qt_spB if pname == "q" else kt_spB
                        P.dma("sp", lambda e, m=m, nb=nb, ob=ob, dst_sp=dst_sp: e.dma_start(
                            dst_sp[m * 128:(m + 1) * 128, nb * 512:(nb + 1) * 512], ob[:]),
                            reads=[obB], writes=[dst_B[m][nb]],
                            sem=(f"dqs{m % 2}" if pname == "q" else f"dks{m % 2}"))
                        if pname == "k":
                            ob2 = outb[m % 2]
                            ob2B = outbB[m % 2]
                            P.op("ge", lambda e, m=m, h0=h0, ob2=ob2, t1_=tmp1[m % 4]:
                                 e.tensor_tensor(
                                ob2[:].rearrange("p (h j) -> p h j", h=4),
                                t1_[:].rearrange("p (h j) -> p h j", h=4),
                                r_tm[:, m * H + h0:m * H + h0 + 4][:, :, None]
                                .broadcast_to([128, 4, 128]), ALU.mult),
                                reads=[tmp1B[m % 4], beta_cB[m]], writes=[ob2B])
                            P.dma("sp", lambda e, m=m, nb=nb, ob2=ob2: e.dma_start(
                                kh_sp[m * 128:(m + 1) * 128, nb * 512:(nb + 1) * 512], ob2[:]),
                                reads=[ob2B], writes=[kh_spB[m][nb]], sem=f"dhs{m % 2}")
        P.barrier()
        if not cfg.flat:
            sA2.close()

        with contextlib.ExitStack() as s3:
            qTg = sb("qTg", [128, HG * L], BF16, stack=s3, is_transpose_dest=True)
            kTg = sb("kTg", [128, HG * L], BF16, stack=s3, is_transpose_dest=True)
            vch = [sb(f"vch{i}", [128, HG * 128], BF16, stack=s3) for i in range(2)]
            hch = [sb(f"hch{i}", [128, HG * 128], BF16, stack=s3) for i in range(2)]
            st_f = sb("st_f", [128, H * 129], F32, stack=s3)
            st_b = sb("st_b", [128, H * 129], BF16, stack=s3)
            Am = [sb(f"Am{i}", [128, HG * 128], BF16, stack=s3) for i in range(2)]
            dsb = [sb(f"dsb{i}", [128, 8], F32, stack=s3) for i in range(2)]
            rrow = [sb(f"rrow{i}", [HG, 128], F32, stack=s3) for i in range(2)]
            nsb = [sb(f"nsb{i}", [128, HG * 128], F32, stack=s3) for i in range(2)]
            ytb = [sb(f"ytb{i}", [128, HG * 128], BF16, stack=s3) for i in range(2)]
            wgT = [sb(f"wgT{i}", [128, KT * FS], BF16, stack=s3, is_transpose_dest=True)
                   for i in range(2)]
            wuT = [sb(f"wuT{i}", [128, KT * FS], BF16, stack=s3, is_transpose_dest=True)
                   for i in range(2)]
            gsb = [sb(f"gsb{i}", [128, 512], BF16, stack=s3) for i in range(2)]
            gub = [sb(f"gub{i}", [128, 512], BF16, stack=s3) for i in range(2)]
            wgTB = P.bufs("wgT", 2)
            wuTB = P.bufs("wuT", 2)
            gsbB = P.bufs("gsb", 2)
            gubB = P.bufs("gub", 2)
            nft = FS // 128
            ffn_jobs = [(s, t, tb) for s in range(NS) for t in range(nft) for tb in range(TB)]
            def emit_span_loads(s):
                sl = s % 2
                need_pair = (s * FS) // 512
                emit_preps_until(2 * need_pair + 10)
                P.dma_group("sp", [
                    (lambda e, k=k, s=s, sl=sl: e.dma_start(
                        wgT[sl][:, k * FS:(k + 1) * FS],
                        wg_sp[s * FS:(s + 1) * FS, k * 128:(k + 1) * 128], transpose=True))
                    for k in range(KT)],
                    reads=[wgsp_B[i] for i in range((s * FS) // RPC_GU,
                                                    ((s + 1) * FS + RPC_GU - 1) // RPC_GU)],
                    writes=[wgTB[sl]], sem=f"dwtg{sl}")
                P.dma_group("sp", [
                    (lambda e, k=k, s=s, sl=sl: e.dma_start(
                        wuT[sl][:, k * FS:(k + 1) * FS],
                        wu_sp[s * FS:(s + 1) * FS, k * 128:(k + 1) * 128], transpose=True))
                    for k in range(KT)],
                    reads=[wusp_B[i] for i in range((s * FS) // RPC_GU,
                                                    ((s + 1) * FS + RPC_GU - 1) // RPC_GU)],
                    writes=[wuTB[sl]], sem=f"dwtu{sl}")

            def emit_ffn(n):
                for _ in range(n):
                    if cur["ffn"] >= len(ffn_jobs):
                        return
                    s, t, tb = ffn_jobs[cur["ffn"]]
                    cur["ffn"] += 1
                    while cur["span"] < min(s + 1, NS - 1):
                        emit_span_loads(cur["span"] + 1)
                        cur["span"] += 1
                    sl = s % 2
                    ii = cur["ii"]
                    cur["ii"] = 1 - ii
                    ht = s * nft + t
                    P.mm_group([(lambda e, k=k, t=t, tb=tb, sl=sl: e.matmul(
                        pb[6][:, 0:512], wgT[sl][:, k * FS + t * 128:k * FS + (t + 1) * 128],
                        xnT[:, k * L + tb * 512:k * L + (tb + 1) * 512],
                        start=(k == 0), stop=(k == KT - 1))) for k in range(KT)],
                        reads=[wgTB[sl], xnT_half_tok(tb * 512)], writes=[pbB[6]])
                    P.op("se", lambda e, ii=ii: e.activation(
                        gsb[ii][:], pb[6][:, 0:512], AF.Silu),
                        reads=[pbB[6]], writes=[gsbB[ii]])
                    P.mm_group([(lambda e, k=k, t=t, tb=tb, sl=sl: e.matmul(
                        pb[7][:, 0:512], wuT[sl][:, k * FS + t * 128:k * FS + (t + 1) * 128],
                        xnT[:, k * L + tb * 512:k * L + (tb + 1) * 512],
                        start=(k == 0), stop=(k == KT - 1))) for k in range(KT)],
                        reads=[wuTB[sl], xnT_half_tok(tb * 512)], writes=[pbB[7]])
                    P.op("ve", lambda e, ii=ii: e.scalar_tensor_tensor(
                        out=gub[ii][:], in0=pb[7][:, 0:512], scalar=1.0, in1=gsb[ii][:],
                        op0=ALU.mult, op1=ALU.mult),
                        reads=[pbB[7], gsbB[ii]], writes=[gubB[ii]])
                    P.dma("se", lambda e, ht=ht, tb=tb, ii=ii: e.dma_start(
                        gu_sp[ht, :, tb * 512:(tb + 1) * 512], gub[ii][:]),
                        reads=[gubB[ii]], writes=[gusp_B[ht][tb]], sem=f"dgu{ii}")

            qTgB, kTgB = P.buf("qTg"), P.buf("kTg")
            vchB = P.bufs("vch", 2)
            hchB = P.bufs("hch", 2)
            stfB = P.buf("stf")
            stbB = P.buf("stb")
            AmB = P.bufs("Am", 2)
            dsbB = P.bufs("dsb", 2)
            rrowB = P.bufs("rrow", 2)
            nsbB = P.bufs("nsb", 2)
            ytbB = P.bufs("ytb", 2)

            P.op("ve", lambda e: e.memset(st_f[:], 0.0), writes=[stfB])
            P.op("ve", lambda e: e.memset(st_b[:], 0.0), writes=[stbB])

            first_spans = [False]
            for g in range(NG):
                P.dma_group("sp", [
                    (lambda e, hl=hl, g=g: e.dma_start(
                        qTg[:, hl * L:(hl + 1) * L],
                        qt_sp[:, (g * HG + hl) * 128:(g * HG + hl + 1) * 128], transpose=True))
                    for hl in range(HG)],
                    reads=[qt_spB[m][g] for m in range(MT)], writes=[qTgB], sem="dqt")
                P.dma_group("sp", [
                    (lambda e, hl=hl, g=g: e.dma_start(
                        kTg[:, hl * L:(hl + 1) * L],
                        kt_sp[:, (g * HG + hl) * 128:(g * HG + hl + 1) * 128], transpose=True))
                    for hl in range(HG)],
                    reads=[kt_spB[m][g] for m in range(MT)], writes=[kTgB], sem="dkt")
                if not first_spans[0]:
                    first_spans[0] = True
                    emit_span_loads(0)
                    if NS > 1:
                        emit_span_loads(1)
                    cur["span"] = min(1, NS - 1)
                for c in range(NC):
                    sl = c % 2
                    cp = c % 2
                    bkA = 0
                    bkN = 1
                    dco = 0 if cp == 0 else 8
                    zco = 32 if cp == 0 else 40
                    Am_, dsb_, rrow_, nsb_, ytb_ = Am[cp], dsb[cp], rrow[cp], nsb[cp], ytb[cp]
                    if c % 2 == 0:
                        emit_preps_until(cur["prep"] + 1)
                    P.dma("se", lambda e, c=c, sl=sl, g=g: e.dma_start(
                        vch[sl][:], v_sp[c * 128:(c + 1) * 128, g * HG * 128:(g + 1) * HG * 128]),
                        reads=[v_spB[c][g]], writes=[vchB[sl]], sem=f"dvc{sl}")
                    P.dma("se", lambda e, c=c, sl=sl, g=g: e.dma_start(
                        hch[sl][:], kh_sp[c * 128:(c + 1) * 128, g * HG * 128:(g + 1) * HG * 128]),
                        reads=[kh_spB[c][g]], writes=[hchB[sl]], sem=f"dhc{sl}")
                    P.mm_group([(lambda e, hl=hl, c=c, bkA=bkA: e.matmul(
                        pb[bkA][:, hl * 128:(hl + 1) * 128],
                        kTg[:, hl * L + c * 128:hl * L + (c + 1) * 128],
                        qTg[:, hl * L + c * 128:hl * L + (c + 1) * 128],
                        start=True, stop=True)) for hl in range(HG)],
                        reads=[qTgB, kTgB], writes=[pbB[bkA]])
                    emit_ffn(1)
                    P.op("ve", lambda e, Am_=Am_, bkA=bkA: e.tensor_tensor(
                        Am_[:].rearrange("p (h j) -> p h j", h=HG),
                        pb[bkA][:, 0:HG * 128].rearrange("p (h j) -> p h j", h=HG),
                        maskf[:, None, :].broadcast_to([128, HG, 128]), ALU.mult),
                        reads=[pbB[bkA], maskB], writes=[AmB[cp]])
                    numer_fns = []
                    for hl in range(HG):
                        h = g * HG + hl
                        numer_fns.append(lambda e, hl=hl, h=h, c=c, bkN=bkN: e.matmul(
                            pb[bkN][:, hl * 128:(hl + 1) * 128],
                            st_b[:, h * 129:h * 129 + 128],
                            qTg[:, hl * L + c * 128:hl * L + (c + 1) * 128],
                            start=True, stop=False))
                        numer_fns.append(lambda e, hl=hl, sl=sl, Am_=Am_, bkN=bkN: e.matmul(
                            pb[bkN][:, hl * 128:(hl + 1) * 128],
                            vch[sl][:, hl * 128:(hl + 1) * 128],
                            Am_[:, hl * 128:(hl + 1) * 128], start=False, stop=True))
                    P.mm_group(numer_fns, reads=[stbB, qTgB, vchB[sl], AmB[cp]], writes=[pbB[bkN]])
                    emit_ffn(1)
                    den_fns = []
                    for hl in range(HG):
                        h = g * HG + hl
                        den_fns.append(lambda e, hl=hl, h=h, c=c, dco=dco: e.matmul(
                            pb[2][:, dco + hl:dco + hl + 1],
                            qTg[:, hl * L + c * 128:hl * L + (c + 1) * 128],
                            st_b[:, h * 129 + 128:h * 129 + 129], start=True, stop=False))
                        den_fns.append(lambda e, hl=hl, Am_=Am_, dco=dco: e.matmul(
                            pb[2][:, dco + hl:dco + hl + 1],
                            Am_[:, hl * 128:(hl + 1) * 128],
                            onesb[:, 0:1], start=False, stop=True))
                    P.mm_group(den_fns, reads=[stbB, qTgB, AmB[cp], constsB], writes=[pbB[2]])
                    emit_ffn(1)
                    P.op("ve", lambda e, dsb_=dsb_, dco=dco: e.tensor_scalar(
                        out=dsb_[:, 0:HG], in0=pb[2][:, dco:dco + HG],
                        scalar1=ATT_EPS, scalar2=None, op0=ALU.add),
                         reads=[pbB[2]], writes=[dsbB[cp]])
                    P.op("ve", lambda e, dsb_=dsb_: e.reciprocal(dsb_[:, 4:4 + HG], dsb_[:, 0:HG]),
                         reads=[dsbB[cp]], writes=[dsbB[cp]])
                    P.mm_group([lambda e, dsb_=dsb_: e.transpose(
                        pb[3][0:HG, 0:128], dsb_[:, 4:4 + HG], identf[:])],
                        reads=[dsbB[cp], identB], writes=[pbB[3]])
                    P.op("ve", lambda e, rrow_=rrow_: e.tensor_copy(rrow_[:], pb[3][0:HG, 0:128]),
                         reads=[pbB[3]], writes=[rrowB[cp]])
                    P.mm_group([(lambda e, hl=hl, rrow_=rrow_: e.matmul(
                        pb[4][:, hl * 128:(hl + 1) * 128], sel4[0:HG, hl * 128:(hl + 1) * 128],
                        rrow_[:], start=True, stop=True))
                        for hl in range(HG)],
                        reads=[rrowB[cp], selB], writes=[pbB[4]])
                    emit_ffn(1)
                    P.op("se", lambda e, nsb_=nsb_, bkN=bkN: e.activation(
                        nsb_[:], pb[bkN][:, 0:HG * 128], AF.Copy),
                         reads=[pbB[bkN]], writes=[nsbB[cp]])
                    P.op("ve", lambda e, ytb_=ytb_, nsb_=nsb_: e.tensor_tensor(
                        ytb_[:], nsb_[:], pb[4][:, 0:HG * 128], ALU.mult),
                         reads=[nsbB[cp], pbB[4]], writes=[ytbB[cp]])
                    P.dma("se", lambda e, c=c, g=g, ytb_=ytb_: e.dma_start(
                        yt_sp[c, :, g * HG * 128:(g + 1) * HG * 128], ytb_[:]),
                        reads=[ytbB[cp]], writes=[yt_spB[c][g]], sem=f"dyt{c % 2}")
                    P.mm_group([(lambda e, hl=hl, sl=sl: e.matmul(
                        pb[5][:, hl * 128:(hl + 1) * 128], hch[sl][:, hl * 128:(hl + 1) * 128],
                        vch[sl][:, hl * 128:(hl + 1) * 128], start=True, stop=True))
                        for hl in range(HG)],
                        reads=[hchB[sl], vchB[sl]], writes=[pbB[5]])
                    P.mm_group([(lambda e, hl=hl, sl=sl, zco=zco: e.matmul(
                        pb[2][:, zco + hl:zco + hl + 1], hch[sl][:, hl * 128:(hl + 1) * 128],
                        onesb[:, 0:1], start=True, stop=True)) for hl in range(HG)],
                        reads=[hchB[sl], constsB], writes=[pbB[2]])
                    emit_ffn(1 + (c % 2))
                    h0g = g * HG
                    stS = st_f[:].rearrange("p (h q) -> p h q", q=129)[:, h0g:h0g + HG, 0:128]
                    stz = st_f[:].rearrange("p (h q) -> p h q", q=129)[:, h0g:h0g + HG, 128:129]
                    aCg = aC_all[:, c * H + h0g:c * H + h0g + HG]
                    P.op("ve", lambda e, stS=stS, aCg=aCg: e.tensor_tensor(
                        stS, stS, aCg[:, :, None].broadcast_to([128, HG, 128]), ALU.mult),
                        reads=[stfB, beta_cB[c]], writes=[stfB])
                    P.op("ve", lambda e, stS=stS: e.tensor_tensor(
                        stS, stS, pb[5][:, 0:HG * 128].rearrange("p (h j) -> p h j", h=HG),
                        ALU.add),
                        reads=[stfB, pbB[5]], writes=[stfB])
                    P.op("ve", lambda e, stz=stz, aCg=aCg: e.tensor_tensor(
                        stz, stz, aCg[:, :, None], ALU.mult),
                        reads=[stfB, beta_cB[c]], writes=[stfB])
                    P.op("ve", lambda e, stz=stz, zco=zco: e.tensor_tensor(
                        stz, stz, pb[2][:, zco:zco + HG].rearrange("p (h j) -> p h j", j=1),
                        ALU.add),
                        reads=[stfB, pbB[2]], writes=[stfB])
                    P.op("se", lambda e, g=g: e.activation(
                        st_b[:, g * HG * 129:(g + 1) * HG * 129],
                        st_f[:, g * HG * 129:(g + 1) * HG * 129], AF.Copy),
                        reads=[stfB], writes=[stbB])
            emit_ffn(len(ffn_jobs))
            emit_preps_until(len(prep_jobs))
        P.barrier()

        if not cfg.flat:
            stA.close()
        with contextlib.ExitStack() as s46:
            nhalves0 = max(1, (DM // 512) // 2)
            boW = sb("boW", [128, DM // nhalves0], F32, stack=s46)
            boWB = P.buf("boW")
            woslot = [sb(f"wos{i}", [128, KT * 512], BF16, stack=s46) for i in range(2)]
            woslotB = [[Buf(f"wos{i}h{j}") for j in range(2)] for i in range(2)]
            yTm = [sb(f"yTm{i}", [128, DM], BF16, stack=s46) for i in range(2)]
            yTmB = P.bufs("yTm", 2)
            s_out = [sb(f"s_out{i}", [128, 512], F32, stack=s46) for i in range(2)]
            s_outB = P.bufs("s_out", 2)
            GS = min(2, NDS)
            wdT = [sb(f"wdT{i}", [128, FKT * DDS], BF16, stack=s46, is_transpose_dest=True)
                   for i in range(GS)]
            guTm = [sb(f"guTm{i}", [128, FKT * 256], BF16, stack=s46) for i in range(2)]
            xtl = [sb(f"xtl{i}", [128, DDS], F32, stack=s46) for i in range(2)]
            stl = [sb(f"stl{i}", [128, DDS], F32, stack=s46) for i in range(2)]
            gtl = [sb(f"gtl{i}", [128, DDS], F32, stack=s46) for i in range(2)]
            wdTB = [[Buf(f"wdT{i}h{j}") for j in range(2)] for i in range(GS)]
            guTmB = P.bufs("guTm", 2)
            xtlB = P.bufs("xtl", 2)
            stlB = P.bufs("stl", 2)
            gtlB = P.bufs("gtl", 2)
            nhalves = max(1, NB // 2)
            W2 = DM // nhalves

            def emit_wo_loads(half):
                P.dma("ge", lambda e, half=half: e.dma_start(
                    boW[:], bo_h[:, half * W2:(half + 1) * W2].broadcast_to([128, W2])),
                    writes=[boWB], sem="dc1")
                for nb in range(half * (NB // nhalves), (half + 1) * (NB // nhalves)):
                    sl = nb % 2
                    for j in range(2):
                        P.dma_group("ge", [
                            (lambda e, k=k, nb=nb, sl=sl: e.dma_start(
                                woslot[sl][:, k * 512:(k + 1) * 512],
                                wo_h[k * 128:(k + 1) * 128, nb * 512:(nb + 1) * 512]))
                            for k in range(j * KT // 2, (j + 1) * KT // 2)],
                            writes=[woslotB[sl][j]], sem=f"dwo{sl}h{j}")

            def emit_s4(half, m):
                msl = m % 2
                nbs = list(range(half * (NB // nhalves), (half + 1) * (NB // nhalves)))
                P.dma("se", lambda e, m=m, msl=msl: e.dma_start(yTm[msl][:], yt_sp[m, :, :]),
                      reads=[yt_spB[m][g2] for g2 in range(NG)],
                      writes=[yTmB[msl]], sem=f"dym{msl}")
                for nb in nbs:
                    sl = nb % 2
                    pbank = 6 + nb % 2
                    if m % MT == 0:
                        for j in range(2):
                            P.mm_group([(lambda e, k=k, msl=msl, sl=sl, pbank=pbank: e.matmul(
                                pb[pbank][:, 0:512], yTm[msl][:, k * 128:(k + 1) * 128],
                                woslot[sl][:, k * 512:(k + 1) * 512],
                                start=(k == 0), stop=(k == KT - 1)))
                                for k in range(j * KT // 2, (j + 1) * KT // 2)],
                                reads=[yTmB[msl], woslotB[sl][j]], writes=[pbB[pbank]])
                    else:
                        P.mm_group([(lambda e, k=k, msl=msl, sl=sl, pbank=pbank: e.matmul(
                            pb[pbank][:, 0:512], yTm[msl][:, k * 128:(k + 1) * 128],
                            woslot[sl][:, k * 512:(k + 1) * 512],
                            start=(k == 0), stop=(k == KT - 1))) for k in range(KT)],
                            reads=[yTmB[msl], woslotB[sl][0], woslotB[sl][1]],
                            writes=[pbB[pbank]])
                    P.op("ve", lambda e, nb=nb, pbank=pbank, half=half: e.tensor_tensor(
                        s_out[nb % 2][:], pb[pbank][:, 0:512],
                        boW[:, (nb - half * (NB // nhalves)) * 512:
                            (nb - half * (NB // nhalves) + 1) * 512], ALU.add),
                        reads=[pbB[pbank], boWB], writes=[s_outB[nb % 2]])
                    P.dma("se", lambda e, m=m, nb=nb, sl=sl: e.dma_start(
                        s_sp[m * 128:(m + 1) * 128, nb * 512:(nb + 1) * 512],
                        s_out[nb % 2][:]),
                        reads=[s_outB[nb % 2]], writes=[s_spB[m][half]], sem=f"dss{m % 2}")

            def emit_wd_prep(gg):
                for j in range(GS):
                    n2 = gg * GS + j
                    for jh in range(2):
                        P.dma_group("sp", [
                            (lambda e, kh=kh, n2=n2, j=j: e.dma_start(
                                wdT[j][:, kh * DDS:(kh + 1) * DDS],
                                wd_sp[n2 * DDS:(n2 + 1) * DDS, kh * 128:(kh + 1) * 128],
                                transpose=True))
                            for kh in range(jh * FKT // 2, (jh + 1) * (FKT // 2)
                                            if jh == 0 else FKT)],
                            reads=[wdsp_B[i] for i in range((n2 * DDS) // RPC_D,
                                                           ((n2 + 1) * DDS + RPC_D - 1) // RPC_D)],
                            writes=[wdTB[j][jh]], sem=f"dwd{j}h{jh}")

            def emit_gu_load(mp):
                msl = mp % 2
                nsplit = 4 if FKT > 16 else 1
                step = (FKT + nsplit - 1) // nsplit
                P.dma_group("se", [
                    (lambda e, mp=mp, msl=msl, k0=k0: e.dma_start(
                        guTm[msl][:, k0 * 256:min(FKT, k0 + step) * 256]
                        .rearrange("p (k f) -> p k f", f=256),
                        gu_sp.rearrange("k p f -> p k f")[:, k0:min(FKT, k0 + step),
                                                         mp * 256:(mp + 1) * 256]))
                    for k0 in range(0, FKT, step)],
                    reads=[gusp_B[k][t2] for k in range(FKT) for t2 in range(TB)],
                    writes=[guTmB[msl]], sem=f"dgm{msl}")

            def emit_s6_mp(gg, mp, last):
                msl = mp % 2
                if not last:
                    emit_gu_load((mp + 1) % (MT // 2))
                for mi in range(2):
                    m = mp * 2 + mi
                    for j in range(GS):
                        n2 = gg * GS + j
                        ii = (m * GS + j) % 2
                        pbank = 2 + (m * GS + j) % 4
                        P.dma("sp", lambda e, m=m, n2=n2, ii=ii: e.dma_start(
                            xtl[ii][:], x_h[m * 128:(m + 1) * 128, n2 * DDS:(n2 + 1) * DDS]),
                            writes=[xtlB[ii]], sem=f"dxl{ii}")
                        hf = (n2 * DDS) // max(1, DM // 2) if NB > 2 else 0
                        P.dma("se", lambda e, m=m, n2=n2, ii=ii: e.dma_start(
                            stl[ii][:], s_sp[m * 128:(m + 1) * 128, n2 * DDS:(n2 + 1) * DDS]),
                            reads=[s_spB[m][hf]], writes=[stlB[ii]], sem=f"dsl{ii}")
                        if mp == 0:
                            for jh in range(2):
                                P.mm_group([(lambda e, kh=kh, mi=mi, msl=msl, j=j, pbank=pbank:
                                             e.matmul(
                                    pb[pbank][:, 0:DDS],
                                    guTm[msl][:, kh * 256 + mi * 128:kh * 256 + (mi + 1) * 128],
                                    wdT[j][:, kh * DDS:(kh + 1) * DDS],
                                    start=(kh == 0), stop=(kh == FKT - 1)))
                                    for kh in range(jh * (FKT // 2),
                                                    (jh + 1) * (FKT // 2) if jh == 0 else FKT)],
                                    reads=[guTmB[msl], wdTB[j][jh]], writes=[pbB[pbank]])
                        else:
                            P.mm_group([(lambda e, kh=kh, mi=mi, msl=msl, j=j, pbank=pbank:
                                         e.matmul(
                                pb[pbank][:, 0:DDS],
                                guTm[msl][:, kh * 256 + mi * 128:kh * 256 + (mi + 1) * 128],
                                wdT[j][:, kh * DDS:(kh + 1) * DDS],
                                start=(kh == 0), stop=(kh == FKT - 1))) for kh in range(FKT)],
                                reads=[guTmB[msl], wdTB[j][0], wdTB[j][1]],
                                writes=[pbB[pbank]])
                        P.op("se", lambda e, pbank=pbank, ii=ii: e.activation(
                            gtl[ii][:], pb[pbank][:, 0:DDS], AF.Silu),
                            reads=[pbB[pbank]], writes=[gtlB[ii]])
                        P.op("ve", lambda e, ii=ii: e.tensor_tensor(
                            gtl[ii][:], gtl[ii][:], stl[ii][:], ALU.mult),
                            reads=[gtlB[ii], stlB[ii]], writes=[gtlB[ii]])
                        P.op("ge", lambda e, ii=ii: e.tensor_tensor(
                            gtl[ii][:], gtl[ii][:], xtl[ii][:], ALU.add),
                            reads=[gtlB[ii], xtlB[ii]], writes=[gtlB[ii]])
                        P.dma("sp", lambda e, m=m, n2=n2, ii=ii: e.dma_start(
                            y_h[m * 128:(m + 1) * 128, n2 * DDS:(n2 + 1) * DDS], gtl[ii][:]),
                            reads=[gtlB[ii]], writes=[], sem=f"dout{ii}")

            emit_wo_loads(0)
            for gg in range(NDS // GS):
                emit_wd_prep(gg)
                if gg == 0:
                    emit_gu_load(0)
                for m in range(MT):
                    if gg < nhalves:
                        emit_s4(gg, m)
                        if m == MT - 1 and gg + 1 < nhalves:
                            emit_wo_loads(gg + 1)
                    if m % 2 == 1:
                        emit_s6_mp(gg, (m - 1) // 2,
                                   gg == NDS // GS - 1 and m == MT - 1)
        P.final_wait("sp", "dout0")
        P.final_wait("sp", "dout1")

        with nc.Block() as blk:
            P.replay(blk)
    return nc


def make_rope_tables(L):
    inv = (1.0 / (10000.0 ** (np.arange(64, dtype=np.float64) / 64.0))).astype(np.float32)
    t = np.arange(L, dtype=np.float32)
    freqs = (t[:, None] * inv[None, :]).astype(np.float32)
    cos = np.cos(freqs.astype(np.float64)).astype(np.float32)
    sin = np.sin(freqs.astype(np.float64)).astype(np.float32)
    return cos, sin


def make_in_maps(cfg, x_full, wq, wk, wv, wg, bg, wo, bo, wqg, sg, wqu, su, wqd, sd):
    cos, sin = make_rope_tables(cfg.L)
    base = dict(
        wq=np.ascontiguousarray(wq, np.float32), wk=np.ascontiguousarray(wk, np.float32),
        wv=np.ascontiguousarray(wv, np.float32), wg=np.ascontiguousarray(wg, np.float32),
        bg=np.ascontiguousarray(np.asarray(bg, np.float32).reshape(cfg.H, 1)),
        wo=np.ascontiguousarray(wo, np.float32),
        bo=np.ascontiguousarray(np.asarray(bo, np.float32).reshape(1, cfg.DM)),
        wqg=np.ascontiguousarray(wqg, np.int32),
        sg=np.ascontiguousarray(np.asarray(sg, np.float32).reshape(cfg.FH // 128, 128).T),
        wqu=np.ascontiguousarray(wqu, np.int32),
        su=np.ascontiguousarray(np.asarray(su, np.float32).reshape(cfg.FH // 128, 128).T),
        wqd=np.ascontiguousarray(wqd, np.int32),
        sd=np.ascontiguousarray(np.asarray(sd, np.float32).reshape(cfg.DM // 128, 128).T),
        rc=cos, rs=sin,
    )
    maps = []
    for b in range(x_full.shape[0]):
        m = dict(base)
        m["x"] = np.ascontiguousarray(x_full[b], np.float32)
        maps.append(m)
    return maps


def kernel(L0_attn_x, L0_attn_Wq, L0_attn_Wk, L0_attn_Wv, L0_attn_Wg, L0_attn_bg,
           L0_attn_Wo, L0_attn_bo, L0_ffn_wq_gate, L0_ffn_s_gate, L0_ffn_wq_up,
           L0_ffn_s_up, L0_ffn_wq_down, L0_ffn_s_down):
    cfg = Cfg(L=2048, DM=2048, FH=5632)
    x = np.asarray(L0_attn_x, dtype=np.float32)
    nc = build_program(cfg)
    in_maps = make_in_maps(cfg, x, L0_attn_Wq, L0_attn_Wk, L0_attn_Wv, L0_attn_Wg,
                           L0_attn_bg, L0_attn_Wo, L0_attn_bo, L0_ffn_wq_gate,
                           L0_ffn_s_gate, L0_ffn_wq_up, L0_ffn_s_up, L0_ffn_wq_down,
                           L0_ffn_s_down)
    res = run_bass_kernel_spmd(nc, in_maps, core_ids=list(range(8)))
    out = np.stack([np.asarray(r["y"], dtype=np.float32) for r in res.results], axis=0)
    return out
